# version 04
# baseline (speedup 1.0000x reference)
import math
import contextlib
import numpy as np
import ml_dtypes
import concourse.bass as bass
import concourse.mybir as mybir
from concourse.bass_utils import run_bass_kernel_spmd

F32 = mybir.dt.float32
BF16 = mybir.dt.bfloat16
FP8 = mybir.dt.float8e4
OP = mybir.AluOpType
AF = mybir.ActivationFunctionType
AXX = mybir.AxisListType.X
BN = ml_dtypes.bfloat16

B, T, D = 4, 8192, 2048
H, HKV, HD = 16, 4, 128
NB = 512
TH = 4096
ST = 512
NST = TH // ST
SUB = 128
NSUB = ST // SUB
DCH = D // 128
EPS = 1e-5
ISQ = 1.0 / math.sqrt(128.0)


class Sm:
    def __init__(self, h):
        self.h = h
        self.n = 0

    def inc(self, inst, k=1):
        inst.then_inc(self.h, k)
        self.n += k
        return (self, self.n)

    def tok(self):
        return (self, self.n)


def W(eng, *toks):
    for t in toks:
        if t is not None:
            s, v = t
            eng.wait_ge(s.h, v)


def _perm_pairpack(nheads):
    perm = []
    for g in range(nheads // 2):
        h0, h1 = 2 * g, 2 * g + 1
        perm += [h0 * HD + 2 * j for j in range(64)]
        perm += [h1 * HD + 2 * j for j in range(64)]
        perm += [h0 * HD + 2 * j + 1 for j in range(64)]
        perm += [h1 * HD + 2 * j + 1 for j in range(64)]
    return np.asarray(perm, dtype=np.int64)


_PQ = _perm_pairpack(H)
_PK = _perm_pairpack(HKV)
_DEINT = np.concatenate([np.arange(0, HD, 2), np.arange(1, HD, 2)])


def _host_tables(half):
    t0 = half * TH
    freqs = (np.float32(1.0) /
             np.array([math.pow(10000.0, i / 64.0) for i in range(64)],
                      dtype=np.float64).astype(np.float32))
    tpos = np.arange(t0, t0 + TH, dtype=np.float32)
    ang = (tpos[:, None] * freqs[None, :]).astype(np.float32)
    cos = np.cos(ang.astype(np.float64)).astype(np.float32)
    sin = np.sin(ang.astype(np.float64)).astype(np.float32)
    coscos = np.ascontiguousarray(np.concatenate([cos.T, cos.T], axis=0))
    sinsin = np.ascontiguousarray(np.concatenate([sin.T, sin.T], axis=0))
    iota64 = np.broadcast_to(np.arange(64, dtype=np.float32)[None, :], (128, 64)).copy()
    LT = np.zeros((128, 2 * NB), dtype=np.float32)
    SELC = np.zeros((128, 2 * NB), dtype=np.float32)
    for c in range(2):
        gi = half * 256 + 128 * c + np.arange(128)
        LT[:, NB * c:NB * c + NB] = (np.arange(NB)[None, :] < gi[:, None])
        SELC[np.arange(128), NB * c + gi] = 1.0
    S8 = np.zeros((128, 8), dtype=np.float32)
    S8[np.arange(128), np.arange(128) // 16] = 1.0
    Sroll = np.zeros((128, 16 * 128), dtype=np.float32)
    for m in range(16):
        Sroll[np.arange(128), 128 * m + 8 * m + np.arange(128) // 16] = 1.0
    ecol = np.zeros((128, 16), dtype=np.float32)
    for c in range(4):
        ecol[:, 4 * c + c] = 1.0
    return dict(coscos=coscos, sinsin=sinsin, iota64=iota64, LT=LT, SELC=SELC,
                S8=S8, S8T=np.ascontiguousarray(S8.T), Sroll=Sroll,
                ident_f=np.eye(128, dtype=np.float32),
                ones_row=np.ones((1, 128), dtype=np.float32), ecol=ecol)


def _host_cwstat(cwa, cwb):
    ca = np.asarray(cwa, dtype=np.float32)[_DEINT]
    cb = np.asarray(cwb, dtype=np.float32)[_DEINT]
    s = np.zeros((128, 8), dtype=np.float32)
    for par in range(2):
        s[0:64, 4 * par + 0] = ca[64 * par:64 * par + 64]
        s[64:128, 4 * par + 1] = ca[64 * par:64 * par + 64]
        s[0:64, 4 * par + 2] = cb[64 * par:64 * par + 64]
        s[64:128, 4 * par + 3] = cb[64 * par:64 * par + 64]
    return s


def build_program(dbg=False, scr=False):
    nc = bass.Bass("TRN2", target_bir_lowering=False, debug=False)
    KS = "ExternalOutput" if (dbg or scr) else "Internal"

    d_x = nc.dram_tensor("x", [TH, D], F32, kind="ExternalInput").ap()
    d_xf = nc.dram_tensor("xf", [512, D], F32, kind="ExternalInput").ap()
    d_wq = nc.dram_tensor("wqp", [D, D], F32, kind="ExternalInput").ap()
    d_wk = nc.dram_tensor("wkp", [D, 512], F32, kind="ExternalInput").ap()
    d_wv = nc.dram_tensor("wv", [D, 512], F32, kind="ExternalInput").ap()
    d_wo = nc.dram_tensor("wo", [D, D], F32, kind="ExternalInput").ap()
    d_wiq = nc.dram_tensor("wiq", [D, 128], F32, kind="ExternalInput").ap()
    d_wik = nc.dram_tensor("wik", [D, 128], F32, kind="ExternalInput").ap()
    d_fw = nc.dram_tensor("fw", [D, D], F32, kind="ExternalInput").ap()
    d_cw = nc.dram_tensor("cwstat", [128, 8], F32, kind="ExternalInput").ap()
    d_cc = nc.dram_tensor("coscos", [128, TH], F32, kind="ExternalInput").ap()
    d_ss = nc.dram_tensor("sinsin", [128, TH], F32, kind="ExternalInput").ap()
    d_iota = nc.dram_tensor("iota64", [128, 64], F32, kind="ExternalInput").ap()
    d_LT = nc.dram_tensor("LT", [128, 2 * NB], F32, kind="ExternalInput").ap()
    d_SELC = nc.dram_tensor("SELC", [128, 2 * NB], F32, kind="ExternalInput").ap()
    d_S8 = nc.dram_tensor("S8", [128, 8], F32, kind="ExternalInput").ap()
    d_S8T = nc.dram_tensor("S8T", [8, 128], F32, kind="ExternalInput").ap()
    d_Sroll = nc.dram_tensor("Sroll", [128, 16 * 128], F32, kind="ExternalInput").ap()
    d_idf = nc.dram_tensor("ident_f", [128, 128], F32, kind="ExternalInput").ap()
    d_ones = nc.dram_tensor("ones_row", [1, 128], F32, kind="ExternalInput").ap()
    d_ecol = nc.dram_tensor("ecol", [128, 16], F32, kind="ExternalInput").ap()

    d_out = nc.dram_tensor("out", [TH, D], F32, kind="ExternalOutput").ap()

    s_xh1 = nc.dram_tensor("xh1_scr", [TH, D], BF16, kind=KS).ap()
    s_att = nc.dram_tensor("att_scr", [TH, H * 64], BF16, kind=KS).ap()
    s_ao = nc.dram_tensor("ao_scr", [TH, D], BF16, kind=KS).ap()
    s_xd = nc.dram_tensor("xd_scr", [TH, D], BF16, kind=KS).ap()
    s_wd = nc.dram_tensor("wd_scr", [D, D], BF16, kind=KS).ap()
    d_dbg = nc.dram_tensor("dbg_scr", [512, 2048], F32, kind=KS).ap() if dbg else None
    d_dbg2 = nc.dram_tensor("dbg2_scr", [512, 2048], F32, kind=KS).ap() if dbg else None
    b_cs_i = nc.dram_tensor("cc_cs_in", [4, 512], F32).ap()
    b_cs_o = nc.dram_tensor("cc_cs_out", [8, 512], F32).ap()
    b_sel_i = nc.dram_tensor("cc_sel_in", [64, 1024], F32).ap()
    b_sel_o = nc.dram_tensor("cc_sel_out", [128, 1024], F32).ap()

    est = contextlib.ExitStack()
    with est:
        sq = Sm(est.enter_context(nc.semaphore("s_sync")))
        sg = Sm(est.enter_context(nc.semaphore("s_gp")))
        sT = Sm(est.enter_context(nc.semaphore("s_T")))
        sV = Sm(est.enter_context(nc.semaphore("s_V")))
        sS = Sm(est.enter_context(nc.semaphore("s_S")))
        sC = Sm(est.enter_context(nc.semaphore("s_CC")))

        wbuf = est.enter_context(nc.sbuf_tensor("wbuf", [128, DCH * 2048], BF16,
                                                is_transpose_dest=True))
        c_cw = est.enter_context(nc.sbuf_tensor("c_cw", [128, 8], BF16))
        c_idf = est.enter_context(nc.sbuf_tensor("c_idf", [128, 128], F32))
        c_idb = est.enter_context(nc.sbuf_tensor("c_idb", [128, 128], BF16))
        c_ones = est.enter_context(nc.sbuf_tensor("c_ones", [1, 128], F32))
        c_S8f = est.enter_context(nc.sbuf_tensor("c_S8f", [128, 8], F32))
        c_S8T = est.enter_context(nc.sbuf_tensor("c_S8T", [8, 128], F32))
        ckacc = est.enter_context(nc.sbuf_tensor("ckacc", [128, 2 * 512], F32))
        cvacc = est.enter_context(nc.sbuf_tensor("cvacc", [128, 2 * 512], F32))
        movab = est.enter_context(nc.sbuf_tensor("movab", [128, 8 * 128], BF16))
        cv2 = est.enter_context(nc.sbuf_tensor("cv2", [128, 512], BF16))

        p0 = est.enter_context(nc.psum_tensor("p0", [128, 512], F32))
        p1 = est.enter_context(nc.psum_tensor("p1", [128, 512], F32))
        p2 = est.enter_context(nc.psum_tensor("p2", [128, 512], F32))
        p3 = est.enter_context(nc.psum_tensor("p3", [128, 512], F32))
        ps2 = est.enter_context(nc.psum_tensor("ps2", [128, 1024], F32))
        pt8 = est.enter_context(nc.psum_tensor("pt8", [128, 1024], BF16))
        pg = est.enter_context(nc.psum_tensor("pg", [128, 512], F32))
        PQ = [p0, p1, p2, p3]

        block = est.enter_context(nc.Block())

        def barrier():
            toks = [sq.tok(), sg.tok(), sT.tok(), sV.tok(), sS.tok(), sC.tok()]
            for e in (nc.sync, nc.tensor, nc.vector, nc.scalar, nc.gpsimd):
                W(e, *toks)

        for out, in_ in [(c_idf, d_idf), (c_ones, d_ones), (c_S8f, d_S8),
                         (c_S8T, d_S8T)]:
            sq.inc(nc.sync.dma_start(out=out[:], in_=in_[:]), 16)
        sg.inc(nc.gpsimd.dma_start(out=c_cw[:], in_=d_cw[:]), 16)
        sg.inc(nc.gpsimd.dma_start(out=c_idb[:], in_=d_idf[:]), 16)
        tok_consts = sq.tok()
        tok_gpc = sg.tok()

        with contextlib.ExitStack() as st0:
            fwst = st0.enter_context(nc.sbuf_tensor("fwst", [128, 2 * 2048], F32))
            fq8 = st0.enter_context(nc.sbuf_tensor("fq8", [128, 2048], FP8))
            fbf = st0.enter_context(nc.sbuf_tensor("fbf", [128, 2048], BF16))
            fam = st0.enter_context(nc.sbuf_tensor("fam", [128, 16], F32))
            famT = st0.enter_context(nc.sbuf_tensor("famT", [16, 128], F32))
            fsc = st0.enter_context(nc.sbuf_tensor("fsc", [16, 8], F32))
            fsr = st0.enter_context(nc.sbuf_tensor("fsr", [1, 32], F32))
            fscB = st0.enter_context(nc.sbuf_tensor("fscB", [128, 32], F32))
            ld = [None] * DCH
            tq_toks = [None] * DCH
            tok_evac = None
            tok_st = None
            W(nc.tensor, tok_consts, tok_gpc)
            for r in range(2):
                ld[r] = sq.inc(nc.sync.dma_start(
                    out=fwst[:, 2048 * (r % 2):2048 * (r % 2) + 2048],
                    in_=d_fw[128 * r:128 * r + 128, :]), 16)
            for r in range(DCH):
                buf = fwst[:, 2048 * (r % 2):2048 * (r % 2) + 2048]
                W(nc.vector, ld[r], tok_consts)
                nc.vector.reduce_max(
                    fam[:], buf.rearrange("p (g k) -> p g k", g=16), axis=AXX,
                    apply_absolute_value=True)
                tam = sV.inc(nc.vector.drain())
                W(nc.tensor, tam, tok_evac)
                ttr = sT.inc(nc.tensor.transpose(pg[0:16, 0:128], fam[:], c_idf[:]))
                W(nc.vector, ttr)
                sV.inc(nc.vector.tensor_copy(famT[:, :], pg[0:16, 0:128]))
                nc.vector.drain()
                sV.inc(nc.vector.reduce_max(fsc[:, 0:1], famT[:], axis=AXX))
                nc.vector.drain()
                sV.inc(nc.vector.tensor_scalar(fsc[:, 1:2], fsc[:, 0:1], 1e-4, None, OP.max))
                nc.vector.drain()
                sV.inc(nc.vector.reciprocal(fsc[:, 2:3], fsc[:, 1:2]))
                nc.vector.drain()
                sV.inc(nc.vector.tensor_scalar(fsc[:, 3:4], fsc[:, 2:3], 224.0, None, OP.mult))
                nc.vector.drain()
                nc.vector.tensor_scalar(fsc[:, 4:5], fsc[:, 1:2],
                                        1.0 / 224.0, None, OP.mult)
                tsc = sV.inc(nc.vector.drain())
                W(nc.tensor, tsc)
                sT.inc(nc.tensor.transpose(pg[0:1, 128:144], fsc[:, 3:4],
                                           c_idf[0:16, 0:16]))
                ttr2 = sT.inc(nc.tensor.transpose(pg[0:1, 176:192], fsc[:, 4:5],
                                                  c_idf[0:16, 0:16]))
                W(nc.vector, ttr2)
                nc.vector.tensor_copy(fsr[:, 0:16], pg[0:1, 128:144])
                nc.vector.tensor_copy(fsr[:, 16:32], pg[0:1, 176:192])
                tvr = sV.inc(nc.vector.drain())
                W(nc.tensor, tvr)
                sT.inc(nc.tensor.matmul(pg[0:128, 144:160], c_ones[:], fsr[0:1, 0:16],
                                        start=True, stop=True))
                tb = sT.inc(nc.tensor.matmul(pg[0:128, 160:176], c_ones[:], fsr[0:1, 16:32],
                                             start=True, stop=True))
                W(nc.vector, tb)
                nc.vector.tensor_copy(fscB[:, 0:16], pg[0:128, 144:160])
                nc.vector.tensor_copy(fscB[:, 16:32], pg[0:128, 160:176])
                tok_evac = sV.inc(nc.vector.drain())
                W(nc.vector, tok_st)
                sV.inc(nc.vector.tensor_tensor(
                    fq8[:].rearrange("p (g k) -> p g k", g=16),
                    buf.rearrange("p (g k) -> p g k", g=16),
                    fscB[:, 0:16].unsqueeze(-1).broadcast_to((128, 16, 128)), OP.mult))
                tq = sV.inc(nc.vector.tensor_tensor(
                    fbf[:].rearrange("p (g k) -> p g k", g=16),
                    fq8[:].rearrange("p (g k) -> p g k", g=16),
                    fscB[:, 16:32].unsqueeze(-1).broadcast_to((128, 16, 128)), OP.mult))
                tq_toks[r] = tq
                W(nc.sync, tq)
                tok_st = sq.inc(nc.sync.dma_start(out=s_wd[128 * r:128 * r + 128, :],
                                                  in_=fbf[:]), 16)
                if r + 2 < DCH:
                    ld[r + 2] = sq.inc(nc.sync.dma_start(
                        out=fwst[:, 2048 * (r % 2):2048 * (r % 2) + 2048],
                        in_=d_fw[128 * (r + 2):128 * (r + 2) + 128, :]), 16)
        barrier()

        tok_colsum_mm = None
        with contextlib.ExitStack() as st1:
            for d in range(DCH):
                sg.inc(nc.gpsimd.dma_start(out=wbuf[:, 512 * d:512 * d + 512],
                                           in_=d_wk[128 * d:128 * d + 128, :]), 16)
                sg.inc(nc.gpsimd.dma_start(
                    out=wbuf[:, 8192 + 512 * d:8192 + 512 * d + 512],
                    in_=d_wv[128 * d:128 * d + 128, :]), 16)
            tok_wkv = sg.tok()

            xbuf = st1.enter_context(nc.sbuf_tensor("xbuf", [128, 4 * 2048], F32))
            xh1f = st1.enter_context(nc.sbuf_tensor("xh1f", [128, 2 * 2048], F32))
            sqd = st1.enter_context(nc.sbuf_tensor("sqd", [128, 2048], F32))
            statS = st1.enter_context(nc.sbuf_tensor("statS", [128, 4], F32))
            statQ = st1.enter_context(nc.sbuf_tensor("statQ", [128, 4], F32))
            statM = st1.enter_context(nc.sbuf_tensor("statM", [128, 16], F32))
            xfm = st1.enter_context(nc.sbuf_tensor("xfm", [128, DCH * 512], BF16,
                                                   is_transpose_dest=True))
            kro = st1.enter_context(nc.sbuf_tensor("kro", [128, 4 * 512], BF16))
            krt1 = st1.enter_context(nc.sbuf_tensor("krt1", [128, 2 * 512], BF16))
            vfm = st1.enter_context(nc.sbuf_tensor("vfm", [128, 4 * 512], BF16))
            eab = st1.enter_context(nc.sbuf_tensor("eab", [128, 8], F32))
            gsb = st1.enter_context(nc.sbuf_tensor("gsb", [8, 24], F32))
            usb = st1.enter_context(nc.sbuf_tensor("usb", [128, 8], F32))
            wsb = st1.enter_context(nc.sbuf_tensor("wsb", [128, 4], F32))
            kwvw = st1.enter_context(nc.sbuf_tensor("kwvw", [128, 1024], BF16))
            c_Sroll = st1.enter_context(nc.sbuf_tensor("c_Sroll", [128, 16 * 128], BF16))
            c_ecol = st1.enter_context(nc.sbuf_tensor("c_ecol", [128, 16], F32))
            tcos = st1.enter_context(nc.sbuf_tensor("tcos", [128, 2 * ST], F32))
            tsin = st1.enter_context(nc.sbuf_tensor("tsin", [128, 2 * ST], F32))
            csacc = st1.enter_context(nc.sbuf_tensor("csacc", [4, 512], F32))

            sg.inc(nc.gpsimd.dma_start(out=c_Sroll[:], in_=d_Sroll[:]), 16)
            tok_sroll = sg.tok()
            sq.inc(nc.sync.dma_start(out=c_ecol[:], in_=d_ecol[:]), 16)
            tok_ecol = sq.tok()

            colsum_started = [False]
            dbgdma = {}
            csaddV = [None]
            finq = [None]
            prev_w_tok = [None]
            prev_ck_evac = [None]
            prev_sab_user = [None]
            prev_texp = [None]
            tok_tab = [None] * NST
            tok_xload = [[None] * NSUB for _ in range(NST)]
            tok_scr = [None] * NST
            tok_xfm = [None] * NST
            tok_ropeV = [None] * NST
            tok_kvT = [None] * NST
            normS = [None] * (TH // SUB)
            colsumT = [None] * (TH // SUB)
            scrgp = [None] * (TH // SUB)

            for t in range(NST):
                par = t % 2
                if t >= 2:
                    W(nc.sync, tok_ropeV[t - 2])
                sq.inc(nc.sync.dma_start(out=tcos[:, ST * par:ST * par + ST],
                                         in_=d_cc[:, ST * t:ST * t + ST]), 16)
                tok_tab[t] = sq.inc(nc.sync.dma_start(out=tsin[:, ST * par:ST * par + ST],
                                                      in_=d_ss[:, ST * t:ST * t + ST]), 16)
                for s in range(NSUB):
                    u = NSUB * t + s
                    if u >= 4:
                        W(nc.sync, normS[u - 4])
                    tok_xload[t][s] = sq.inc(nc.sync.dma_start(
                        out=xbuf[:, 2048 * (u % 4):2048 * (u % 4) + 2048],
                        in_=d_x[SUB * u:SUB * u + SUB, :]), 16)
                for s in range(NSUB):
                    u = NSUB * t + s
                    bx = xbuf[:, 2048 * (u % 4):2048 * (u % 4) + 2048]
                    W(nc.vector, tok_xload[t][s])
                    sV.inc(nc.vector.tensor_scalar(bx, bx, 1.0, None, OP.mult, OP.add,
                                                   accum_out=statS[:, s:s + 1]))
                    W(nc.scalar, tok_xload[t][s], finq[0])
                    sS.inc(nc.scalar.activation(sqd[:], bx, AF.Square,
                                                accum_out=statQ[:, s:s + 1]))
                sS.inc(nc.scalar.drain())
                nc.vector.drain()
                W(nc.vector, sS.tok())
                sV.inc(nc.vector.tensor_scalar(statM[:, 0:4], statS[:], 1.0 / D, None, OP.mult))
                nc.vector.drain()
                finq[0] = sV.inc(nc.vector.tensor_scalar(statM[:, 4:8], statQ[:], 1.0 / D,
                                                          None, OP.mult))
                nc.vector.drain()
                sV.inc(nc.vector.tensor_tensor(statM[:, 8:12], statM[:, 0:4],
                                               statM[:, 0:4], OP.mult))
                nc.vector.drain()
                tvv = sV.inc(nc.vector.tensor_tensor(statM[:, 8:12], statM[:, 4:8],
                                                     statM[:, 8:12], OP.subtract))
                nc.vector.drain()
                tvv = sV.inc(nc.vector.tensor_scalar(statM[:, 8:12], statM[:, 8:12],
                                                     EPS, None, OP.add))
                nc.vector.drain()
                W(nc.scalar, tvv)
                nc.scalar.activation(statM[:, 4:8], statM[:, 8:12], AF.Sqrt)
                tsd = sS.inc(nc.scalar.drain())
                W(nc.vector, tsd)
                nc.vector.reciprocal(statM[:, 8:12], statM[:, 4:8])
                trv = sV.inc(nc.vector.drain())
                nc.vector.scalar_tensor_tensor(
                    statM[:, 12:16], statM[:, 0:4], -1.0, statM[:, 8:12], OP.mult, OP.mult)
                tmb = sV.inc(nc.vector.drain())
                for s in range(NSUB):
                    u = NSUB * t + s
                    bx = xbuf[:, 2048 * (u % 4):2048 * (u % 4) + 2048]
                    bh = xh1f[:, 2048 * (u % 2):2048 * (u % 2) + 2048]
                    W(nc.vector, trv)
                    if u >= 2:
                        W(nc.vector, colsumT[u - 2], scrgp[u - 2], dbgdma.get(u - 2))
                    tvsub = sV.inc(nc.vector.tensor_scalar(
                        bh, bx, statM[:, 0 + s:1 + s], None, OP.subtract))
                    W(nc.scalar, tvsub)
                    th = sS.inc(nc.scalar.activation(bh, bh, AF.Identity,
                                                     scale=statM[:, 8 + s:9 + s]))
                    normS[u] = th
                    W(nc.tensor, th, tok_ecol, csaddV[0])
                    for c in range(4):
                        tk = sT.inc(nc.tensor.matmul(
                            pg[0:4, 0:512], c_ecol[:, 4 * c:4 * c + 4],
                            bh[:, 512 * c:512 * c + 512], start=(c == 0), stop=(c == 3)))
                        colsumT[u] = tk
                    W(nc.vector, colsumT[u])
                    if u == 0:
                        nc.vector.tensor_copy(csacc[:, :], pg[0:4, 0:512])
                    else:
                        nc.vector.tensor_tensor(
                            csacc[:, :], csacc[:, :], pg[0:4, 0:512], OP.add)
                    csaddV[0] = sV.inc(nc.vector.drain())
                    if u == TH // SUB - 1:
                        tok_colsum_mm = csaddV[0]
                    if dbg and s == NSUB - 1:
                        W(nc.sync, csaddV[0])
                        sq.inc(nc.sync.dma_start(out=d_dbg[404 + 4 * t:408 + 4 * t, 0:512],
                                                 in_=csacc[:, :]), 16)
                    W(nc.gpsimd, th, colsumT[u])
                    scrgp[u] = tok_scr[t] = sg.inc(nc.gpsimd.dma_start(
                        out=s_xh1[SUB * u:SUB * u + SUB, :], in_=bh), 16)
                    if dbg and t == 3:
                        W(nc.sync, th)
                        dbgdma[u] = sq.inc(nc.sync.dma_start(
                            out=d_dbg2[SUB * s:SUB * s + SUB, :], in_=bh), 16)
                W(nc.sync, tok_scr[t])
                if t >= 1:
                    W(nc.sync, tok_kvT[t - 1])
                for c in range(DCH):
                    tok_xfm[t] = sq.inc(nc.sync.dma_start(
                        out=xfm[:, 512 * c:512 * c + 512],
                        in_=s_xh1[ST * t:ST * t + ST, 128 * c:128 * c + 128],
                        transpose=True), 16)
                W(nc.tensor, tok_wkv, tok_xfm[t], prev_w_tok[0], prev_sab_user[0])
                for half in range(2):
                    tA = tB = None
                    for d in range(DCH):
                        o = 512 * d + 128 * (2 * half)
                        tA = sT.inc(nc.tensor.matmul(
                            p2[:, :], wbuf[:, o:o + 128], xfm[:, 512 * d:512 * d + 512],
                            start=(d == 0), stop=(d == DCH - 1)))
                    for d in range(DCH):
                        o = 512 * d + 128 * (2 * half + 1)
                        tB = sT.inc(nc.tensor.matmul(
                            p3[:, :], wbuf[:, o:o + 128], xfm[:, 512 * d:512 * d + 512],
                            start=(d == 0), stop=(d == DCH - 1)))
                    W(nc.vector, tA, tB, tok_tab[t])
                    cc_ = tcos[:, ST * par:ST * par + ST]
                    ss_ = tsin[:, ST * par:ST * par + ST]
                    a0 = 512 * (2 * half)
                    b0 = 512 * (2 * half + 1)
                    sV.inc(nc.vector.tensor_tensor(krt1[:, 0:512], p2[:, :], cc_, OP.mult))
                    sV.inc(nc.vector.tensor_tensor(krt1[:, 512:1024], p3[:, :], ss_, OP.mult))
                    sV.inc(nc.vector.tensor_tensor(kro[:, a0:a0 + 512], krt1[:, 0:512],
                                                   krt1[:, 512:1024], OP.subtract))
                    sV.inc(nc.vector.tensor_tensor(krt1[:, 0:512], p2[:, :], ss_, OP.mult))
                    sV.inc(nc.vector.tensor_tensor(krt1[:, 512:1024], p3[:, :], cc_, OP.mult))
                    trope = sV.inc(nc.vector.tensor_tensor(
                        kro[:, b0:b0 + 512], krt1[:, 0:512], krt1[:, 512:1024], OP.add))
                    tok_ropeV[t] = trope
                    W(nc.tensor, trope)
                for half in range(2):
                    tA = tB = None
                    for d in range(DCH):
                        o = 8192 + 512 * d + 128 * (2 * half)
                        tA = sT.inc(nc.tensor.matmul(
                            ps2[:, 0:512], wbuf[:, o:o + 128], xfm[:, 512 * d:512 * d + 512],
                            start=(d == 0), stop=(d == DCH - 1)))
                    for d in range(DCH):
                        o = 8192 + 512 * d + 128 * (2 * half + 1)
                        tB = sT.inc(nc.tensor.matmul(
                            ps2[:, 512:1024], wbuf[:, o:o + 128],
                            xfm[:, 512 * d:512 * d + 512],
                            start=(d == 0), stop=(d == DCH - 1)))
                    tok_kvT[t] = tB
                    W(nc.scalar, tA, tB)
                    sS.inc(nc.scalar.copy(vfm[:, 512 * (2 * half):512 * (2 * half) + 512],
                                          ps2[:, 0:512]))
                    tvc = sS.inc(nc.scalar.copy(
                        vfm[:, 512 * (2 * half + 1):512 * (2 * half + 1) + 512],
                        ps2[:, 512:1024]))
                    W(nc.tensor, tvc)
                for s in range(NSUB):
                    u = NSUB * t + s
                    tile = u // 16
                    mslot = u % 16
                    mm_tok = None
                    W(nc.tensor, tok_gpc, prev_texp[0])
                    for g2 in range(2):
                        for parb in range(2):
                            ch = 2 * g2 + parb
                            mm_tok = sT.inc(nc.tensor.matmul(
                                ps2[:, 4 * g2:4 * g2 + 4],
                                kro[:, 512 * ch + 128 * s:512 * ch + 128 * s + 128],
                                c_cw[:, 4 * parb:4 * parb + 4],
                                start=(g2 == 0 and parb == 0),
                                stop=(g2 == 1 and parb == 1)))
                    W(nc.scalar, mm_tok, prev_sab_user[0])
                    nc.scalar.activation(eab[:, :], ps2[:, 0:8], AF.Exp)
                    texp = sS.inc(nc.scalar.drain())
                    prev_texp[0] = texp
                    W(nc.tensor, texp, prev_sab_user[0])
                    tgs = sT.inc(nc.tensor.matmul(ps2[0:8, 512:520], c_S8f[:], eab[:, :],
                                                  start=True, stop=True))
                    W(nc.vector, tgs)
                    sV.inc(nc.vector.tensor_copy(gsb[:, 0:8], ps2[0:8, 512:520]))
                    nc.vector.drain()
                    sV.inc(nc.vector.reciprocal(gsb[:, 8:16], gsb[:, 0:8]))
                    nc.vector.drain()
                    nc.vector.tensor_scalar(gsb[:, 16:24], gsb[:, 8:16],
                                            0.5, None, OP.mult)
                    tgr = sV.inc(nc.vector.drain())
                    W(nc.tensor, tgr)
                    trt = sT.inc(nc.tensor.matmul(ps2[0:128, 520:528], c_S8T[:],
                                                  gsb[:, 16:24], start=True, stop=True))
                    W(nc.vector, trt)
                    sV.inc(nc.vector.tensor_tensor(usb[:, :], eab[:, :], ps2[:, 520:528],
                                                   OP.mult))
                    nc.vector.drain()
                    uview = usb[:].rearrange("p (g ab h) -> p g ab h", g=2, ab=2, h=2)
                    nc.vector.tensor_tensor(
                        wsb[:].rearrange("p (g h) -> p g h", g=2),
                        uview[:, :, 0, :], uview[:, :, 1, :], OP.add)
                    tw = sV.inc(nc.vector.drain())
                    prev_sab_user[0] = tw
                    W(nc.tensor, prev_w_tok[0])
                    ttr = None
                    for c in range(4):
                        ttr = sT.inc(nc.tensor.transpose(
                            pt8[:, 128 * c:128 * c + 128],
                            kro[:, 512 * c + 128 * s:512 * c + 128 * s + 128], c_idb[:]))
                    for c in range(4):
                        ttr = sT.inc(nc.tensor.transpose(
                            pt8[:, 512 + 128 * c:512 + 128 * c + 128],
                            vfm[:, 512 * c + 128 * s:512 * c + 128 * s + 128], c_idb[:]))
                    W(nc.vector, ttr, tw)
                    wk_view = wsb[:].rearrange("p (g h) -> p g h", g=2)
                    wkb = wk_view.unsqueeze(3).broadcast_to((128, 2, 2, 64))
                    for ab in range(2):
                        sV.inc(nc.vector.tensor_tensor(
                            kwvw[:, 0:512].rearrange("p (g ab h j) -> p g ab h j",
                                                     g=2, ab=2, h=2)[:, :, ab, :, :],
                            pt8[:, 0:512].rearrange("p (g ab h j) -> p g ab h j",
                                                    g=2, ab=2, h=2)[:, :, ab, :, :],
                            wkb, OP.mult))
                    wvb = wsb[:].unsqueeze(2).broadcast_to((128, 4, 128))
                    twv = sV.inc(nc.vector.tensor_tensor(
                        kwvw[:, 512:1024].rearrange("p (h j) -> p h j", h=4),
                        pt8[:, 512:1024].rearrange("p (h j) -> p h j", h=4),
                        wvb, OP.mult))
                    prev_w_tok[0] = twv
                    W(nc.tensor, twv, prev_ck_evac[0], tok_sroll)
                    stat = c_Sroll[:, 128 * mslot:128 * mslot + 128]
                    sT.inc(nc.tensor.matmul(p0[:, :], stat, kwvw[:, 0:512],
                                            start=(mslot == 0), stop=(mslot == 15)))
                    tcv = sT.inc(nc.tensor.matmul(p1[:, :], stat, kwvw[:, 512:1024],
                                                  start=(mslot == 0), stop=(mslot == 15)))
                    if mslot == 15:
                        W(nc.vector, tcv)
                        sV.inc(nc.vector.tensor_copy(ckacc[:, 512 * tile:512 * tile + 512],
                                                     p0[:, :]))
                        prev_ck_evac[0] = sV.inc(nc.vector.tensor_copy(
                            cvacc[:, 512 * tile:512 * tile + 512], p1[:, :]))
            W(nc.sync, tok_colsum_mm)
            tok_colsum_mm = sq.inc(nc.sync.dma_start(out=b_cs_i[:, :], in_=csacc[:, :]), 16)
            if dbg:
                sq.inc(nc.sync.dma_start(out=d_dbg[400:404, 0:512], in_=csacc[:, :]), 16)
        barrier()

        with contextlib.ExitStack() as st15:
            csb = st15.enter_context(nc.sbuf_tensor("csb", [4, 1024], F32))
            cst = st15.enter_context(nc.sbuf_tensor("cst", [4, 512], F32))
            csfm = st15.enter_context(nc.sbuf_tensor("csfm", [128, 16], F32))
            wikf = st15.enter_context(nc.sbuf_tensor("wikf", [128, DCH * 128], F32))
            wiqf = st15.enter_context(nc.sbuf_tensor("wiqf", [128, DCH * 128], F32))
            xfb = st15.enter_context(nc.sbuf_tensor("xfb", [128, 4 * 2048], F32))
            xffm = st15.enter_context(nc.sbuf_tensor("xffm", [128, DCH * 512], F32))
            kmb = st15.enter_context(nc.sbuf_tensor("kmb", [128, 2], F32))
            qif = st15.enter_context(nc.sbuf_tensor("qif", [128, 512], F32))
            srow = st15.enter_context(nc.sbuf_tensor("srow", [1, 512], F32))
            sbb = st15.enter_context(nc.sbuf_tensor("sbb", [128, 512], F32))
            tmpa = st15.enter_context(nc.sbuf_tensor("tmpa", [128, 2048], F32))
            scol = st15.enter_context(nc.sbuf_tensor("scol", [128, 4], F32))
            rnk = st15.enter_context(nc.sbuf_tensor("rnk", [128, 8], F32))
            ptb = st15.enter_context(nc.sbuf_tensor("ptb", [128, 128], F32))
            selb = st15.enter_context(nc.sbuf_tensor("selb", [64, 1024], F32))
            selt = st15.enter_context(nc.sbuf_tensor("selt", [64, 2048], F32))
            st_q = st15.enter_context(nc.sbuf_tensor("st_q", [128, 8], F32))
            st_s = st15.enter_context(nc.sbuf_tensor("st_s", [128, 16], F32))
            c_iota = st15.enter_context(nc.sbuf_tensor("c_iota", [128, 64], F32))
            c_LT = st15.enter_context(nc.sbuf_tensor("c_LT", [128, 2 * NB], F32))
            c_SELC = st15.enter_context(nc.sbuf_tensor("c_SELC", [128, 2 * NB], F32))

            for out, in_ in [(c_iota, d_iota), (c_LT, d_LT), (c_SELC, d_SELC)]:
                sq.inc(nc.sync.dma_start(out=out[:], in_=in_[:]), 16)
            for d in range(DCH):
                sq.inc(nc.sync.dma_start(out=wikf[:, 128 * d:128 * d + 128],
                                         in_=d_wik[128 * d:128 * d + 128, :]), 16)
                sq.inc(nc.sync.dma_start(out=wiqf[:, 128 * d:128 * d + 128],
                                         in_=d_wiq[128 * d:128 * d + 128, :]), 16)
            for s in range(4):
                sq.inc(nc.sync.dma_start(out=xfb[:, 2048 * s:2048 * s + 2048],
                                         in_=d_xf[128 * s:128 * s + 128, :]), 16)
            tok_lds = sq.tok()

            W(nc.gpsimd, tok_colsum_mm)
            tcc = sC.inc(nc.gpsimd.collective_compute(
                "AllGather", OP.bypass, replica_groups=[[0, 1], [2, 3], [4, 5], [6, 7]],
                ins=[b_cs_i.opt()], outs=[b_cs_o.opt()]))
            W(nc.sync, tcc)
            sq.inc(nc.sync.dma_start(out=csb[:, 0:512], in_=b_cs_o[0:4, :]), 16)
            tcr = sq.inc(nc.sync.dma_start(out=csb[:, 512:1024], in_=b_cs_o[4:8, :]), 16)
            W(nc.vector, tcr)
            tct = sV.inc(nc.vector.tensor_tensor(cst[:, :], csb[:, 0:512],
                                                 csb[:, 512:1024], OP.add))
            W(nc.tensor, tct)
            ttj = None
            for m in range(4):
                ttj = sT.inc(nc.tensor.transpose(
                    pg[0:128, 4 * m:4 * m + 4], cst[0:4, 128 * m:128 * m + 128],
                    c_idf[0:4, 0:4]))
            W(nc.vector, ttj)
            nc.vector.tensor_copy(csfm[:, :], pg[0:128, 0:16])
            tcf = sV.inc(nc.vector.drain())
            W(nc.tensor, tcf, tok_lds)
            tkm = None
            for d in range(DCH):
                col = 4 * (d % 4) + d // 4
                tkm = sT.inc(nc.tensor.matmul(ps2[0:128, 0:1],
                                              wikf[:, 128 * d:128 * d + 128],
                                              csfm[:, col:col + 1], start=(d == 0),
                                              stop=(d == DCH - 1)))
            W(nc.scalar, tkm)
            nc.scalar.activation(kmb[:, 0:1], ps2[:, 0:1], AF.Identity, scale=1.0 / T)
            tkme = sS.inc(nc.scalar.drain())
            for s in range(4):
                bx = xfb[:, 2048 * s:2048 * s + 2048]
                W(nc.vector, tok_lds)
                sV.inc(nc.vector.tensor_scalar(bx, bx, 1.0, None, OP.mult, OP.add,
                                               accum_out=st_q[:, s:s + 1]))
                W(nc.scalar, tok_lds)
                sS.inc(nc.scalar.activation(tmpa[:], bx, AF.Square,
                                            accum_out=st_q[:, 4 + s:5 + s]))
            sS.inc(nc.scalar.drain())
            nc.vector.drain()
            W(nc.vector, sS.tok())
            sV.inc(nc.vector.tensor_scalar(st_s[:, 0:4], st_q[:, 0:4], 1.0 / D, None, OP.mult))
            nc.vector.drain()
            sV.inc(nc.vector.tensor_scalar(st_s[:, 4:8], st_q[:, 4:8], 1.0 / D, None, OP.mult))
            nc.vector.drain()
            sV.inc(nc.vector.tensor_tensor(st_s[:, 8:12], st_s[:, 0:4], st_s[:, 0:4],
                                           OP.mult))
            nc.vector.drain()
            tvv = sV.inc(nc.vector.tensor_tensor(st_s[:, 8:12], st_s[:, 4:8],
                                                 st_s[:, 8:12], OP.subtract))
            nc.vector.drain()
            tvv = sV.inc(nc.vector.tensor_scalar(st_s[:, 8:12], st_s[:, 8:12],
                                                 EPS, None, OP.add))
            nc.vector.drain()
            W(nc.scalar, tvv)
            nc.scalar.activation(st_s[:, 4:8], st_s[:, 8:12], AF.Sqrt)
            tsd = sS.inc(nc.scalar.drain())
            W(nc.vector, tsd)
            nc.vector.reciprocal(st_s[:, 8:12], st_s[:, 4:8])
            trv = sV.inc(nc.vector.drain())
            nc.vector.scalar_tensor_tensor(
                st_s[:, 12:16], st_s[:, 0:4], -1.0, st_s[:, 8:12], OP.mult, OP.mult)
            tmb = sV.inc(nc.vector.drain())
            tnorm = None
            for s in range(4):
                bx = xfb[:, 2048 * s:2048 * s + 2048]
                W(nc.vector, trv)
                tvsub = sV.inc(nc.vector.tensor_scalar(
                    bx, bx, st_s[:, 0 + s:1 + s], None, OP.subtract))
                W(nc.scalar, tvsub)
                tnorm = sS.inc(nc.scalar.activation(bx, bx, AF.Identity,
                                                    scale=st_s[:, 8 + s:9 + s]))
            W(nc.tensor, tnorm)
            for c in range(DCH):
                ttj = None
                for s in range(4):
                    ttj = sT.inc(nc.tensor.transpose(
                        pg[0:128, 128 * s:128 * s + 128],
                        xfb[:, 2048 * s + 128 * c:2048 * s + 128 * c + 128], c_idf[:]))
                W(nc.vector, ttj)
                tve = sV.inc(nc.vector.tensor_copy(xffm[:, 512 * c:512 * c + 512],
                                                   pg[0:128, 0:512]))
                W(nc.tensor, tve)
            tqi = None
            W(nc.tensor, tkme)
            for d in range(DCH):
                tqi = sT.inc(nc.tensor.matmul(ps2[:, 0:512],
                                              wiqf[:, 128 * d:128 * d + 128],
                                              xffm[:, 512 * d:512 * d + 512],
                                              start=(d == 0), stop=(d == DCH - 1)))
            W(nc.vector, tqi, tkme)
            tqe = sV.inc(nc.vector.tensor_copy(qif[:, :], ps2[:, 0:512]))
            W(nc.tensor, tqe)
            tsc = sT.inc(nc.tensor.matmul(pg[0:1, 0:512], kmb[:, 0:1], qif[:, :],
                                          start=True, stop=True))
            W(nc.vector, tsc)
            tsr = sV.inc(nc.vector.tensor_copy(srow[:, :], pg[0:1, 0:512]))
            W(nc.tensor, tsr)
            tbb = sT.inc(nc.tensor.matmul(ps2[0:128, 0:512], c_ones[:], srow[:, :],
                                          start=True, stop=True))
            W(nc.vector, tbb)
            tsbb = sV.inc(nc.vector.tensor_copy(sbb[:, :], ps2[:, 0:512]))
            tpt = None
            for c in range(2):
                sV.inc(nc.vector.tensor_tensor(tmpa[:, 0:512], sbb[:, :],
                                               c_SELC[:, 512 * c:512 * c + 512], OP.mult))
                sV.inc(nc.vector.reduce_sum(scol[:, c:c + 1], tmpa[:, 0:512], axis=AXX))
                nc.vector.drain()
                sV.inc(nc.vector.tensor_scalar(tmpa[:, 0:512], sbb[:, :],
                                               scol[:, c:c + 1], None, OP.is_gt))
                sV.inc(nc.vector.tensor_scalar(tmpa[:, 512:1024], sbb[:, :],
                                               scol[:, c:c + 1], None, OP.is_equal))
                sV.inc(nc.vector.tensor_tensor(tmpa[:, 512:1024], tmpa[:, 512:1024],
                                               c_LT[:, 512 * c:512 * c + 512], OP.mult))
                sV.inc(nc.vector.tensor_tensor(tmpa[:, 0:512], tmpa[:, 0:512],
                                               tmpa[:, 512:1024], OP.add))
                sV.inc(nc.vector.reduce_sum(rnk[:, c:c + 1], tmpa[:, 0:512], axis=AXX))
                nc.vector.drain()
                nc.vector.tensor_scalar(ptb[:, 64 * c:64 * c + 64],
                                        c_iota[:], rnk[:, c:c + 1], None, OP.is_equal)
                tpt = sV.inc(nc.vector.drain())
            W(nc.tensor, tpt)
            tgk = None
            for c in range(2):
                tgk = sT.inc(nc.tensor.matmul(ps2[0:64, 0:512], ptb[:, 64 * c:64 * c + 64],
                                              ckacc[:, 512 * c:512 * c + 512],
                                              start=(c == 0), stop=(c == 1)))
            for c in range(2):
                tgk = sT.inc(nc.tensor.matmul(ps2[0:64, 512:1024],
                                              ptb[:, 64 * c:64 * c + 64],
                                              cvacc[:, 512 * c:512 * c + 512],
                                              start=(c == 0), stop=(c == 1)))
            W(nc.vector, tgk)
            sV.inc(nc.vector.tensor_copy(selb[:, 0:512], ps2[0:64, 0:512]))
            tse = sV.inc(nc.vector.tensor_copy(selb[:, 512:1024], ps2[0:64, 512:1024]))
            W(nc.sync, tse)
            tsw = sq.inc(nc.sync.dma_start(out=b_sel_i[:, :], in_=selb[:, :]), 16)
            W(nc.gpsimd, tsw)
            tcc2 = sC.inc(nc.gpsimd.collective_compute(
                "AllGather", OP.bypass, replica_groups=[[0, 1], [2, 3], [4, 5], [6, 7]],
                ins=[b_sel_i.opt()], outs=[b_sel_o.opt()]))
            W(nc.sync, tcc2)
            sq.inc(nc.sync.dma_start(out=selt[:, 0:1024], in_=b_sel_o[0:64, :]), 16)
            tsr2 = sq.inc(nc.sync.dma_start(out=selt[:, 1024:2048], in_=b_sel_o[64:128, :]), 16)
            W(nc.vector, tsr2)
            tst = sV.inc(nc.vector.tensor_tensor(selb[:, :], selt[:, 0:1024],
                                                 selt[:, 1024:2048], OP.add))
            W(nc.gpsimd, tst)
            tmz = sg.inc(nc.gpsimd.memset(movab[:], 0.0))
            sg.inc(nc.gpsimd.dma_start(out=cv2[0:64, :], in_=selb[:, 512:1024]), 16)
            tcv2 = sg.inc(nc.gpsimd.dma_start(out=cv2[64:128, :], in_=selb[:, 512:1024]), 16)
            W(nc.tensor, tst)
            for kv in range(4):
                for pr in range(2):
                    col = 256 * (kv // 2) + 64 * (kv % 2) + 128 * pr
                    ttp = sT.inc(nc.tensor.transpose(
                        pg[0:64, 0:64] if pr == 0 else pg[0:64, 64:128],
                        selb[:, col:col + 64], c_idf[0:64, 0:64]))
                W(nc.vector, ttp, tmz)
                m0 = 128 * (2 * kv + 0)
                m1 = 128 * (2 * kv + 1)
                sV.inc(nc.vector.tensor_copy(movab[0:64, m0:m0 + 64], pg[0:64, 0:64]))
                sV.inc(nc.vector.tensor_copy(movab[64:128, m0 + 64:m0 + 128], pg[0:64, 0:64]))
                sV.inc(nc.vector.tensor_copy(movab[0:64, m1:m1 + 64], pg[0:64, 64:128]))
                nc.vector.tensor_copy(movab[64:128, m1 + 64:m1 + 128], pg[0:64, 64:128])
                tmv = sV.inc(nc.vector.drain())
                W(nc.tensor, tmv)
            W(nc.vector, tcv2)
            if dbg:
                W(nc.sync, tst, tpt)
                sq.inc(nc.sync.dma_start(out=d_dbg[0:64, 0:1024], in_=selb[:, :]), 16)
                sq.inc(nc.sync.dma_start(out=d_dbg[64:65, 0:512], in_=srow[:, :]), 16)
                sq.inc(nc.sync.dma_start(out=d_dbg[66:194, 0:8], in_=rnk[:, :]), 16)
                sq.inc(nc.sync.dma_start(out=d_dbg[66:194, 8:12], in_=scol[:, :]), 16)
                sq.inc(nc.sync.dma_start(out=d_dbg[66:194, 16:144], in_=ptb[:, :]), 16)
                sq.inc(nc.sync.dma_start(out=d_dbg[66:194, 144:160], in_=csfm[:, :]), 16)
                sq.inc(nc.sync.dma_start(out=d_dbg[66:194, 160:162], in_=kmb[:, :]), 16)
                sq.inc(nc.sync.dma_start(out=d_dbg[195:199, 0:512], in_=cst[:, :]), 16)
                sq.inc(nc.sync.dma_start(out=d_dbg[199:199+128, 0:1024], in_=ckacc[:, :]), 16)
        barrier()

        with contextlib.ExitStack() as sta:
            for d in range(DCH):
                sg.inc(nc.gpsimd.dma_start(out=wbuf[:, 2048 * d:2048 * d + 2048],
                                           in_=d_wq[128 * d:128 * d + 128, :]), 16)
            tok_wq = sg.tok()
            xfm2 = sta.enter_context(nc.sbuf_tensor("xfm2", [128, DCH * 512], BF16,
                                                    is_transpose_dest=True))
            qro = sta.enter_context(nc.sbuf_tensor("qro", [128, DCH * 512], BF16))
            qrt = sta.enter_context(nc.sbuf_tensor("qrt", [128, 2 * 512], BF16))
            ef = sta.enter_context(nc.sbuf_tensor("ef", [128, 1024], F32))
            mxb = sta.enter_context(nc.sbuf_tensor("mxb", [128, 48], F32))
            attb = sta.enter_context(nc.sbuf_tensor("attb", [128, 2 * 1024], BF16))
            tcos = sta.enter_context(nc.sbuf_tensor("tcos2", [128, 2 * ST], F32))
            tsin = sta.enter_context(nc.sbuf_tensor("tsin2", [128, 2 * ST], F32))

            prev_soft = [None]
            tok_ropeV2 = [None] * NST
            tok_qT = [None] * NST
            attdma = [None] * (TH // SUB)
            rope_toks = [None, None]
            for t in range(NST):
                par = t % 2
                if t >= 2:
                    W(nc.sync, tok_ropeV2[t - 2])
                sq.inc(nc.sync.dma_start(out=tcos[:, ST * par:ST * par + ST],
                                         in_=d_cc[:, ST * t:ST * t + ST]), 16)
                ttab = sq.inc(nc.sync.dma_start(out=tsin[:, ST * par:ST * par + ST],
                                                in_=d_ss[:, ST * t:ST * t + ST]), 16)
                if t >= 1:
                    W(nc.sync, tok_qT[t - 1])
                txf = None
                for c in range(DCH):
                    txf = sq.inc(nc.sync.dma_start(
                        out=xfm2[:, 512 * c:512 * c + 512],
                        in_=s_xh1[ST * t:ST * t + ST, 128 * c:128 * c + 128],
                        transpose=True), 16)
                W(nc.tensor, tok_wq, txf)
                for g in range(8):
                    pa, pb = (p0, p1) if g % 2 == 0 else (p2, p3)
                    W(nc.tensor, rope_toks[g % 2])
                    tA = tB = None
                    for d in range(DCH):
                        o = 2048 * d + 128 * (2 * g)
                        tA = sT.inc(nc.tensor.matmul(
                            pa[:, :], wbuf[:, o:o + 128], xfm2[:, 512 * d:512 * d + 512],
                            start=(d == 0), stop=(d == DCH - 1)))
                    for d in range(DCH):
                        o = 2048 * d + 128 * (2 * g + 1)
                        tB = sT.inc(nc.tensor.matmul(
                            pb[:, :], wbuf[:, o:o + 128], xfm2[:, 512 * d:512 * d + 512],
                            start=(d == 0), stop=(d == DCH - 1)))
                    W(nc.vector, tA, tB, ttab)
                    cc_ = tcos[:, ST * par:ST * par + ST]
                    ss_ = tsin[:, ST * par:ST * par + ST]
                    a0 = 512 * (2 * g)
                    b0 = 512 * (2 * g + 1)
                    sV.inc(nc.vector.tensor_tensor(qrt[:, 0:512], pa[:, :], cc_, OP.mult))
                    sV.inc(nc.vector.tensor_tensor(qrt[:, 512:1024], pb[:, :], ss_, OP.mult))
                    sV.inc(nc.vector.tensor_tensor(qro[:, a0:a0 + 512], qrt[:, 0:512],
                                                   qrt[:, 512:1024], OP.subtract))
                    sV.inc(nc.vector.tensor_tensor(qrt[:, 0:512], pa[:, :], ss_, OP.mult))
                    sV.inc(nc.vector.tensor_tensor(qrt[:, 512:1024], pb[:, :], cc_, OP.mult))
                    rope_toks[g % 2] = sV.inc(nc.vector.tensor_tensor(
                        qro[:, b0:b0 + 512], qrt[:, 0:512], qrt[:, 512:1024], OP.add))
                    tok_ropeV2[t] = rope_toks[g % 2]
                    tok_qT[t] = tB
                for s in range(NSUB):
                    u = NSUB * t + s
                    W(nc.tensor, rope_toks[0], rope_toks[1], prev_soft[0])
                    tms = None
                    for g in range(8):
                        kv = g // 2
                        for pr in range(2):
                            qc = 512 * (2 * g + pr) + 128 * s
                            first = (g % 4 == 0) and pr == 0
                            last = (g % 4 == 3) and pr == 1
                            tms = sT.inc(nc.tensor.matmul(
                                ps2[:, 128 * g:128 * g + 128],
                                qro[:, qc:qc + 128],
                                movab[:, 128 * (2 * kv + pr):128 * (2 * kv + pr) + 128],
                                start=first, stop=last))
                    W(nc.vector, tms)
                    sV.inc(nc.vector.reduce_max(
                        mxb[:, 0:16], ps2[:].rearrange("p (h j) -> p h j", h=16), axis=AXX))
                    nc.vector.drain()
                    tsub = sV.inc(nc.vector.tensor_tensor(
                        ef[:].rearrange("p (h j) -> p h j", h=16),
                        ps2[:].rearrange("p (h j) -> p h j", h=16),
                        mxb[:, 0:16].unsqueeze(2).broadcast_to((128, 16, 64)), OP.subtract))
                    prev_soft[0] = tsub
                    W(nc.scalar, tsub)
                    tex = sS.inc(nc.scalar.activation(ef[:], ef[:], AF.Exp, scale=ISQ))
                    W(nc.vector, tex)
                    sV.inc(nc.vector.reduce_sum(
                        mxb[:, 16:32], ef[:].rearrange("p (h j) -> p h j", h=16), axis=AXX))
                    nc.vector.drain()
                    sV.inc(nc.vector.reciprocal(mxb[:, 32:48], mxb[:, 16:32]))
                    nc.vector.drain()
                    ab = attb[:, 1024 * (u % 2):1024 * (u % 2) + 1024]
                    if u >= 2:
                        W(nc.vector, attdma[u - 2])
                    tat = sV.inc(nc.vector.tensor_tensor(
                        ab.rearrange("p (h j) -> p h j", h=16),
                        ef[:].rearrange("p (h j) -> p h j", h=16),
                        mxb[:, 32:48].unsqueeze(2).broadcast_to((128, 16, 64)), OP.mult))
                    W(nc.sync, tat)
                    attdma[u] = sq.inc(nc.sync.dma_start(
                        out=s_att[SUB * u:SUB * u + SUB, :], in_=ab), 16)
        barrier()

        with contextlib.ExitStack() as stb:
            for d in range(DCH):
                sg.inc(nc.gpsimd.dma_start(out=wbuf[:, 2048 * d:2048 * d + 2048],
                                           in_=d_wo[128 * d:128 * d + 128, :]), 16)
            tok_wo = sg.tok()
            attin = stb.enter_context(nc.sbuf_tensor("attin", [128, 2 * 1024], BF16))
            attT = stb.enter_context(nc.sbuf_tensor("attT", [128, 8 * 512], BF16))
            vout = stb.enter_context(nc.sbuf_tensor("vout", [128, 16 * 512], BF16))
            aob = stb.enter_context(nc.sbuf_tensor("aob", [128, 2 * 2048], BF16))

            prev_vread = [None]
            prev_aodma = [None]
            prev_tev = [None]
            trT = [None] * (TH // SUB)
            for t in range(NST):
                tld = [None] * NSUB
                for s in range(NSUB):
                    u = NSUB * t + s
                    if u >= 2:
                        W(nc.sync, trT[u - 2])
                    tld[s] = sq.inc(nc.sync.dma_start(
                        out=attin[:, 1024 * (s % 2):1024 * (s % 2) + 1024],
                        in_=s_att[SUB * u:SUB * u + SUB, :]), 16)
                    W(nc.tensor, tld[s], prev_vread[0])
                    ttr = None
                    for g in range(8):
                        ttr = sT.inc(nc.tensor.transpose(
                            pt8[:, 128 * g:128 * g + 128],
                            attin[:, 1024 * (s % 2) + 128 * g:1024 * (s % 2) + 128 * g + 128],
                            c_idb[:]))
                    trT[u] = ttr
                    W(nc.scalar, ttr)
                    dst = attT[:].rearrange("p (g c) -> p g c", g=8)[:, :, 128 * s:128 * s + 128]
                    prev_vread[0] = sS.inc(nc.scalar.copy(
                        dst, pt8[:].rearrange("p (g c) -> p g c", g=8)))
                W(nc.tensor, prev_vread[0], prev_tev[0])
                ve = [None] * 4
                for h in range(H):
                    pq = PQ[h % 4]
                    W(nc.tensor, ve[h % 4])
                    r0 = 64 * (h % 2)
                    tvm = sT.inc(nc.tensor.matmul(
                        pq[:, :], cv2[r0:r0 + 64, 128 * (h // 4):128 * (h // 4) + 128],
                        attT[r0:r0 + 64, 512 * (h // 2):512 * (h // 2) + 512],
                        start=True, stop=True))
                    W(nc.scalar, tvm)
                    ve[h % 4] = sS.inc(nc.scalar.copy(vout[:, 512 * h:512 * h + 512],
                                                      pq[:, :]))
                W(nc.tensor, ve[0], ve[1], ve[2], ve[3], tok_wo)
                for s in range(NSUB):
                    u = NSUB * t + s
                    W(nc.tensor, prev_tev[0])
                    tos = None
                    for oc in range(4):
                        for c in range(DCH):
                            tos = sT.inc(nc.tensor.matmul(
                                PQ[oc][:, :],
                                vout[:, 512 * c + 128 * s:512 * c + 128 * s + 128],
                                wbuf[:, 2048 * c + 512 * oc:2048 * c + 512 * oc + 512],
                                start=(c == 0), stop=(c == DCH - 1)))
                    W(nc.scalar, tos, prev_aodma[0])
                    ao = aob[:, 2048 * (u % 2):2048 * (u % 2) + 2048]
                    tev = None
                    for oc in range(4):
                        tev = sS.inc(nc.scalar.copy(ao[:, 512 * oc:512 * oc + 512],
                                                    PQ[oc][:, :]))
                    prev_tev[0] = tev
                    W(nc.sync, tev)
                    prev_aodma[0] = sq.inc(nc.sync.dma_start(
                        out=s_ao[SUB * u:SUB * u + SUB, :], in_=ao), 16)
        barrier()

        with contextlib.ExitStack() as stc:
            twd = None
            for i in range(DCH):
                twd = sq.inc(nc.sync.dma_start(
                    out=wbuf[:, 2048 * i:2048 * i + 2048],
                    in_=s_wd[:, 128 * i:128 * i + 128], transpose=True), 16)
            aor = stc.enter_context(nc.sbuf_tensor("aor", [128, 2 * 2048], BF16))
            xpr = stc.enter_context(nc.sbuf_tensor("xpr", [128, 8 * 2048], F32))
            xh2 = stc.enter_context(nc.sbuf_tensor("xh2", [128, 2048], F32))
            y8 = stc.enter_context(nc.sbuf_tensor("y8", [128, 2048], FP8))
            xdb = stc.enter_context(nc.sbuf_tensor("xdb", [128, 2048], BF16))
            xdfm = stc.enter_context(nc.sbuf_tensor("xdfm", [128, 2 * DCH * 512], BF16,
                                                    is_transpose_dest=True))
            ffnb = stc.enter_context(nc.sbuf_tensor("ffnb", [128, 2048], BF16))
            outb = stc.enter_context(nc.sbuf_tensor("outb", [128, 2048], F32))
            stS2 = stc.enter_context(nc.sbuf_tensor("stS2", [128, 4], F32))
            stQ2 = stc.enter_context(nc.sbuf_tensor("stQ2", [128, 4], F32))
            stM2 = stc.enter_context(nc.sbuf_tensor("stM2", [128, 16], F32))
            amb = stc.enter_context(nc.sbuf_tensor("amb", [128, 64], F32))

            tok_xd = [None] * NST
            tok_xdfm = [None] * NST
            prev_ffn_read = [None]
            prev_out_add = {}
            prev_ffne = [None]
            prev_outadd = [None]
            xpadd = [None] * (TH // SUB)
            xddma = [None] * (TH // SUB)
            outdma = [None] * (TH // SUB)
            prev_fp8V = [None]

            for t in range(NST + 1):
                if t < NST:
                    tldx = [None] * NSUB
                    tlda = [None] * NSUB
                    for s in range(2):
                        u = NSUB * t + s
                        if u >= 2:
                            W(nc.sync, xpadd[u - 2])
                        W(nc.sync, prev_out_add.get(u % 8))
                        tldx[s] = sq.inc(nc.sync.dma_start(
                            out=xpr[:, 2048 * (u % 8):2048 * (u % 8) + 2048],
                            in_=d_x[SUB * u:SUB * u + SUB, :]), 16)
                        tlda[s] = sq.inc(nc.sync.dma_start(
                            out=aor[:, 2048 * (u % 2):2048 * (u % 2) + 2048],
                            in_=s_ao[SUB * u:SUB * u + SUB, :]), 16)
                    for s in range(NSUB):
                        u = NSUB * t + s
                        slot = u % 8
                        xp = xpr[:, 2048 * slot:2048 * slot + 2048]
                        W(nc.vector, tldx[s], tlda[s])
                        sV.inc(nc.vector.tensor_tensor(
                            xp, xp,
                            aor[:, 2048 * (u % 2):2048 * (u % 2) + 2048], OP.add))
                        tst1 = sV.inc(nc.vector.tensor_scalar(
                            xp, xp, 1.0, None, OP.mult, OP.add,
                            accum_out=stS2[:, s:s + 1]))
                        xpadd[u] = tst1
                        if s + 2 < NSUB:
                            u2 = u + 2
                            W(nc.sync, xpadd[u])
                            W(nc.sync, prev_out_add.get(u2 % 8))
                            tldx[s + 2] = sq.inc(nc.sync.dma_start(
                                out=xpr[:, 2048 * (u2 % 8):2048 * (u2 % 8) + 2048],
                                in_=d_x[SUB * u2:SUB * u2 + SUB, :]), 16)
                            tlda[s + 2] = sq.inc(nc.sync.dma_start(
                                out=aor[:, 2048 * (u2 % 2):2048 * (u2 % 2) + 2048],
                                in_=s_ao[SUB * u2:SUB * u2 + SUB, :]), 16)
                        W(nc.scalar, tst1, prev_fp8V[0])
                        sS.inc(nc.scalar.activation(xh2[:], xp, AF.Square,
                                                    accum_out=stQ2[:, s:s + 1]))
                    sS.inc(nc.scalar.drain())
                    nc.vector.drain()
                    W(nc.vector, sS.tok())
                    sV.inc(nc.vector.tensor_scalar(stM2[:, 0:4], stS2[:], 1.0 / D, None,
                                                   OP.mult))
                    nc.vector.drain()
                    sV.inc(nc.vector.tensor_scalar(stM2[:, 4:8], stQ2[:], 1.0 / D, None,
                                                   OP.mult))
                    nc.vector.drain()
                    sV.inc(nc.vector.tensor_tensor(stM2[:, 8:12], stM2[:, 0:4],
                                                   stM2[:, 0:4], OP.mult))
                    nc.vector.drain()
                    tvv = sV.inc(nc.vector.tensor_tensor(stM2[:, 8:12], stM2[:, 4:8],
                                                         stM2[:, 8:12], OP.subtract))
                    nc.vector.drain()
                    tvv = sV.inc(nc.vector.tensor_scalar(stM2[:, 8:12], stM2[:, 8:12],
                                                         EPS, None, OP.add))
                    nc.vector.drain()
                    W(nc.scalar, tvv)
                    nc.scalar.activation(stM2[:, 4:8], stM2[:, 8:12], AF.Sqrt)
                    tsd = sS.inc(nc.scalar.drain())
                    W(nc.vector, tsd)
                    nc.vector.reciprocal(stM2[:, 8:12], stM2[:, 4:8])
                    trv = sV.inc(nc.vector.drain())
                    nc.vector.scalar_tensor_tensor(
                        stM2[:, 12:16], stM2[:, 0:4], -1.0, stM2[:, 8:12],
                        OP.mult, OP.mult)
                    tmb = sV.inc(nc.vector.drain())
                    for s in range(NSUB):
                        u = NSUB * t + s
                        slot = u % 8
                        xp = xpr[:, 2048 * slot:2048 * slot + 2048]
                        W(nc.vector, trv)
                        W(nc.vector, prev_fp8V[0])
                        tvsub = sV.inc(nc.vector.tensor_scalar(
                            xh2[:], xp, stM2[:, 0 + s:1 + s], None, OP.subtract))
                        W(nc.scalar, tvsub)
                        th2 = sS.inc(nc.scalar.activation(xh2[:], xh2[:], AF.Identity,
                                                          scale=stM2[:, 8 + s:9 + s]))
                        W(nc.vector, th2)
                        if u >= 1:
                            W(nc.vector, xddma[u - 1])
                        sV.inc(nc.vector.reduce_max(
                            amb[:, 0:16], xh2[:].rearrange("p (g k) -> p g k", g=16),
                            axis=AXX, apply_absolute_value=True))
                        nc.vector.drain()
                        sV.inc(nc.vector.tensor_scalar(amb[:, 0:16], amb[:, 0:16],
                                                       1e-4, None, OP.max))
                        nc.vector.drain()
                        sV.inc(nc.vector.reciprocal(amb[:, 16:32], amb[:, 0:16]))
                        nc.vector.drain()
                        sV.inc(nc.vector.tensor_scalar(amb[:, 16:32], amb[:, 16:32],
                                                       224.0, None, OP.mult))
                        nc.vector.drain()
                        sV.inc(nc.vector.tensor_scalar(amb[:, 32:48], amb[:, 0:16],
                                                       1.0 / 224.0, None, OP.mult))
                        nc.vector.drain()
                        sV.inc(nc.vector.tensor_tensor(
                            y8[:].rearrange("p (g k) -> p g k", g=16),
                            xh2[:].rearrange("p (g k) -> p g k", g=16),
                            amb[:, 16:32].unsqueeze(2).broadcast_to((128, 16, 128)),
                            OP.mult))
                        xd = xdb[:, :]
                        txd = sV.inc(nc.vector.tensor_tensor(
                            xd.rearrange("p (g k) -> p g k", g=16),
                            y8[:].rearrange("p (g k) -> p g k", g=16),
                            amb[:, 32:48].unsqueeze(2).broadcast_to((128, 16, 128)),
                            OP.mult))
                        prev_fp8V[0] = txd
                        W(nc.sync, txd)
                        xddma[u] = tok_xd[t] = sq.inc(nc.sync.dma_start(
                            out=s_xd[SUB * u:SUB * u + SUB, :], in_=xd), 16)
                    W(nc.sync, tok_xd[t], prev_ffn_read[0])
                    fmo = DCH * 512 * (t % 2)
                    for c in range(DCH):
                        tok_xdfm[t] = sq.inc(nc.sync.dma_start(
                            out=xdfm[:, fmo + 512 * c:fmo + 512 * c + 512],
                            in_=s_xd[ST * t:ST * t + ST, 128 * c:128 * c + 128],
                            transpose=True), 16)
                tf = t - 1
                if tf >= 0:
                    W(nc.tensor, tok_xdfm[tf], twd)
                    ffo = DCH * 512 * (tf % 2)
                    for s in range(NSUB):
                        u = NSUB * tf + s
                        slot = u % 8
                        W(nc.tensor, prev_ffne[0])
                        tos = None
                        for oc in range(4):
                            for i in range(DCH):
                                tos = sT.inc(nc.tensor.matmul(
                                    PQ[oc][:, :],
                                    xdfm[:, ffo + 512 * i + 128 * s:ffo + 512 * i + 128 * s + 128],
                                    wbuf[:, 2048 * i + 512 * oc:2048 * i + 512 * oc + 512],
                                    start=(i == 0), stop=(i == DCH - 1)))
                        W(nc.scalar, tos, prev_outadd[0])
                        tev = None
                        for oc in range(4):
                            tev = sS.inc(nc.scalar.copy(ffnb[:, 512 * oc:512 * oc + 512],
                                                        PQ[oc][:, :]))
                        prev_ffne[0] = tev
                        ob = outb[:, :]
                        W(nc.vector, tev)
                        if u >= 1:
                            W(nc.vector, outdma[u - 1])
                        tadd = sV.inc(nc.vector.tensor_tensor(
                            ob, xpr[:, 2048 * slot:2048 * slot + 2048], ffnb[:], OP.add))
                        prev_out_add[slot] = tadd
                        prev_outadd[0] = tadd
                        W(nc.sync, tadd)
                        outdma[u] = sq.inc(nc.sync.dma_start(
                            out=d_out[SUB * u:SUB * u + SUB, :], in_=ob), 16)
                    prev_ffn_read[0] = (sT, sT.n)
            W(nc.sync, sq.tok())
            W(nc.vector, sq.tok())
    return nc


_CACHE = {}


def kernel(**inputs):
    x = np.asarray(inputs["L0_attn_x"])
    wq = np.asarray(inputs["L0_attn_wq"])[:, _PQ]
    wk = np.asarray(inputs["L0_attn_wk"])[:, _PK]
    wv = np.asarray(inputs["L0_attn_wv"])
    wo = np.asarray(inputs["L0_attn_wo"])
    wiq = np.asarray(inputs["L0_attn_wiq"])
    wik = np.asarray(inputs["L0_attn_wik"])
    fw = np.asarray(inputs["L0_ffn_weight"])
    cwstat = _host_cwstat(inputs["L0_attn_cwa"], inputs["L0_attn_cwb"])

    if "nc" not in _CACHE:
        _CACHE["nc"] = build_program()
        _CACHE["tabs"] = [_host_tables(h) for h in range(2)]
    nc = _CACHE["nc"]
    tabs = _CACHE["tabs"]

    in_maps = []
    for c in range(8):
        b, h = c // 2, c % 2
        tab = tabs[h]
        m = dict(
            x=np.ascontiguousarray(x[b, h * TH:(h + 1) * TH, :]),
            xf=np.ascontiguousarray(x[b, 0:512, :]),
            wqp=np.ascontiguousarray(wq), wkp=np.ascontiguousarray(wk),
            wv=wv, wo=wo, wiq=wiq, wik=wik, fw=fw, cwstat=cwstat,
            coscos=tab["coscos"], sinsin=tab["sinsin"], iota64=tab["iota64"],
            LT=tab["LT"], SELC=tab["SELC"], S8=tab["S8"], S8T=tab["S8T"],
            Sroll=tab["Sroll"], ident_f=tab["ident_f"], ones_row=tab["ones_row"],
            ecol=tab["ecol"],
        )
        in_maps.append(m)

    res = run_bass_kernel_spmd(nc, in_maps, core_ids=list(range(8)))
    out = np.empty((B, T, D), dtype=np.float32)
    for c in range(8):
        b, h = c // 2, c % 2
        out[b, h * TH:(h + 1) * TH, :] = res.results[c]["out"]
    return out


# version 07
# speedup vs baseline: 1.0390x; 1.0390x over previous
import math
import contextlib
import numpy as np
import ml_dtypes
import concourse.bass as bass
import concourse.mybir as mybir
from concourse.bass_utils import run_bass_kernel_spmd

F32 = mybir.dt.float32
BF16 = mybir.dt.bfloat16
FP8 = mybir.dt.float8e4
OP = mybir.AluOpType
AF = mybir.ActivationFunctionType
AXX = mybir.AxisListType.X
BN = ml_dtypes.bfloat16

B, T, D = 4, 8192, 2048
H, HKV, HD = 16, 4, 128
NB = 512
TH = 4096
ST = 512
NST = TH // ST
SUB = 128
NSUB = ST // SUB
DCH = D // 128
EPS = 1e-5
ISQ = 1.0 / math.sqrt(128.0)


class Sm:
    def __init__(self, h):
        self.h = h
        self.n = 0

    def inc(self, inst, k=1):
        inst.then_inc(self.h, k)
        self.n += k
        return (self, self.n)

    def tok(self):
        return (self, self.n)


def W(eng, *toks):
    for t in toks:
        if t is not None:
            s, v = t
            eng.wait_ge(s.h, v)


def _perm_pairpack(nheads):
    perm = []
    for g in range(nheads // 2):
        h0, h1 = 2 * g, 2 * g + 1
        perm += [h0 * HD + 2 * j for j in range(64)]
        perm += [h1 * HD + 2 * j for j in range(64)]
        perm += [h0 * HD + 2 * j + 1 for j in range(64)]
        perm += [h1 * HD + 2 * j + 1 for j in range(64)]
    return np.asarray(perm, dtype=np.int64)


_PQ = _perm_pairpack(H)
_PK = _perm_pairpack(HKV)
_DEINT = np.concatenate([np.arange(0, HD, 2), np.arange(1, HD, 2)])


def _host_tables(half):
    t0 = half * TH
    freqs = (np.float32(1.0) /
             np.array([math.pow(10000.0, i / 64.0) for i in range(64)],
                      dtype=np.float64).astype(np.float32))
    tpos = np.arange(t0, t0 + TH, dtype=np.float32)
    ang = (tpos[:, None] * freqs[None, :]).astype(np.float32)
    cos = np.cos(ang.astype(np.float64)).astype(np.float32)
    sin = np.sin(ang.astype(np.float64)).astype(np.float32)
    coscos = np.ascontiguousarray(np.concatenate([cos.T, cos.T], axis=0))
    sinsin = np.ascontiguousarray(np.concatenate([sin.T, sin.T], axis=0))
    iota64 = np.broadcast_to(np.arange(64, dtype=np.float32)[None, :], (128, 64)).copy()
    LT = np.zeros((128, 2 * NB), dtype=np.float32)
    SELC = np.zeros((128, 2 * NB), dtype=np.float32)
    for c in range(2):
        gi = half * 256 + 128 * c + np.arange(128)
        LT[:, NB * c:NB * c + NB] = (np.arange(NB)[None, :] < gi[:, None])
        SELC[np.arange(128), NB * c + gi] = 1.0
    S8 = np.zeros((128, 8), dtype=np.float32)
    S8[np.arange(128), np.arange(128) // 16] = 1.0
    Sroll = np.zeros((128, 16 * 128), dtype=np.float32)
    for m in range(16):
        Sroll[np.arange(128), 128 * m + 8 * m + np.arange(128) // 16] = 1.0
    ecol = np.zeros((128, 16), dtype=np.float32)
    for c in range(4):
        ecol[:, 4 * c + c] = 1.0
    return dict(coscos=coscos, sinsin=sinsin, iota64=iota64, LT=LT, SELC=SELC,
                S8=S8, S8T=np.ascontiguousarray(S8.T), Sroll=Sroll,
                ident_f=np.eye(128, dtype=np.float32),
                ones_row=np.ones((1, 128), dtype=np.float32), ecol=ecol)


def _host_cwstat(cwa, cwb):
    ca = np.asarray(cwa, dtype=np.float32)[_DEINT]
    cb = np.asarray(cwb, dtype=np.float32)[_DEINT]
    s = np.zeros((128, 8), dtype=np.float32)
    for par in range(2):
        s[0:64, 4 * par + 0] = ca[64 * par:64 * par + 64]
        s[64:128, 4 * par + 1] = ca[64 * par:64 * par + 64]
        s[0:64, 4 * par + 2] = cb[64 * par:64 * par + 64]
        s[64:128, 4 * par + 3] = cb[64 * par:64 * par + 64]
    return s


def build_program(dbg=False, scr=False):
    nc = bass.Bass("TRN2", target_bir_lowering=False, debug=False)
    KS = "ExternalOutput" if (dbg or scr) else "Internal"

    d_x = nc.dram_tensor("x", [TH, D], F32, kind="ExternalInput").ap()
    d_xf = nc.dram_tensor("xf", [512, D], F32, kind="ExternalInput").ap()
    d_wq = nc.dram_tensor("wqp", [D, D], F32, kind="ExternalInput").ap()
    d_wk = nc.dram_tensor("wkp", [D, 512], F32, kind="ExternalInput").ap()
    d_wv = nc.dram_tensor("wv", [D, 512], F32, kind="ExternalInput").ap()
    d_wo = nc.dram_tensor("wo", [D, D], F32, kind="ExternalInput").ap()
    d_wiq = nc.dram_tensor("wiq", [D, 128], F32, kind="ExternalInput").ap()
    d_wik = nc.dram_tensor("wik", [D, 128], F32, kind="ExternalInput").ap()
    d_fw = nc.dram_tensor("fw", [D, D], F32, kind="ExternalInput").ap()
    d_cw = nc.dram_tensor("cwstat", [128, 8], F32, kind="ExternalInput").ap()
    d_cc = nc.dram_tensor("coscos", [128, TH], F32, kind="ExternalInput").ap()
    d_ss = nc.dram_tensor("sinsin", [128, TH], F32, kind="ExternalInput").ap()
    d_iota = nc.dram_tensor("iota64", [128, 64], F32, kind="ExternalInput").ap()
    d_LT = nc.dram_tensor("LT", [128, 2 * NB], F32, kind="ExternalInput").ap()
    d_SELC = nc.dram_tensor("SELC", [128, 2 * NB], F32, kind="ExternalInput").ap()
    d_S8 = nc.dram_tensor("S8", [128, 8], F32, kind="ExternalInput").ap()
    d_S8T = nc.dram_tensor("S8T", [8, 128], F32, kind="ExternalInput").ap()
    d_Sroll = nc.dram_tensor("Sroll", [128, 16 * 128], F32, kind="ExternalInput").ap()
    d_idf = nc.dram_tensor("ident_f", [128, 128], F32, kind="ExternalInput").ap()
    d_ones = nc.dram_tensor("ones_row", [1, 128], F32, kind="ExternalInput").ap()
    d_ecol = nc.dram_tensor("ecol", [128, 16], F32, kind="ExternalInput").ap()

    d_out = nc.dram_tensor("out", [TH, D], F32, kind="ExternalOutput").ap()

    s_xh1 = nc.dram_tensor("xh1_scr", [TH, D], BF16, kind=KS).ap()
    s_att = nc.dram_tensor("att_scr", [TH, H * 64], BF16, kind=KS).ap()
    s_ao = nc.dram_tensor("ao_scr", [TH, D], BF16, kind=KS).ap()
    s_xd = nc.dram_tensor("xd_scr", [TH, D], BF16, kind=KS).ap()
    s_wd = nc.dram_tensor("wd_scr", [D, D], BF16, kind=KS).ap()
    d_dbg = nc.dram_tensor("dbg_scr", [512, 2048], F32, kind=KS).ap() if dbg else None
    d_dbg2 = nc.dram_tensor("dbg2_scr", [512, 2048], F32, kind=KS).ap() if dbg else None
    b_cs_i = nc.dram_tensor("cc_cs_in", [4, 512], F32).ap()
    b_cs_o = nc.dram_tensor("cc_cs_out", [8, 512], F32).ap()
    b_sel_i = nc.dram_tensor("cc_sel_in", [64, 1024], F32).ap()
    b_sel_o = nc.dram_tensor("cc_sel_out", [128, 1024], F32).ap()

    est = contextlib.ExitStack()
    with est:
        sq = Sm(est.enter_context(nc.semaphore("s_sync")))
        sg = Sm(est.enter_context(nc.semaphore("s_gp")))
        sT = Sm(est.enter_context(nc.semaphore("s_T")))
        sV = Sm(est.enter_context(nc.semaphore("s_V")))
        sS = Sm(est.enter_context(nc.semaphore("s_S")))
        sC = Sm(est.enter_context(nc.semaphore("s_CC")))

        wbuf = est.enter_context(nc.sbuf_tensor("wbuf", [128, DCH * 2048], BF16,
                                                is_transpose_dest=True))
        c_cw = est.enter_context(nc.sbuf_tensor("c_cw", [128, 8], BF16))
        c_idf = est.enter_context(nc.sbuf_tensor("c_idf", [128, 128], F32))
        c_idb = est.enter_context(nc.sbuf_tensor("c_idb", [128, 128], BF16))
        c_ones = est.enter_context(nc.sbuf_tensor("c_ones", [1, 128], F32))
        c_S8f = est.enter_context(nc.sbuf_tensor("c_S8f", [128, 8], F32))
        c_S8T = est.enter_context(nc.sbuf_tensor("c_S8T", [8, 128], F32))
        movab = est.enter_context(nc.sbuf_tensor("movab", [128, 8 * 128], BF16))
        cv2 = est.enter_context(nc.sbuf_tensor("cv2", [128, 512], BF16))
        stCK = contextlib.ExitStack()
        ckacc = stCK.enter_context(nc.sbuf_tensor("ckacc", [128, 2 * 512], F32))
        cvacc = stCK.enter_context(nc.sbuf_tensor("cvacc", [128, 2 * 512], F32))

        p0 = est.enter_context(nc.psum_tensor("p0", [128, 512], F32))
        p1 = est.enter_context(nc.psum_tensor("p1", [128, 512], F32))
        p2 = est.enter_context(nc.psum_tensor("p2", [128, 512], F32))
        p3 = est.enter_context(nc.psum_tensor("p3", [128, 512], F32))
        ps2 = est.enter_context(nc.psum_tensor("ps2", [128, 1024], F32))
        pt8 = est.enter_context(nc.psum_tensor("pt8", [128, 1024], BF16))
        pg = est.enter_context(nc.psum_tensor("pg", [128, 512], F32))
        PQ = [p0, p1, p2, p3]

        block = est.enter_context(nc.Block())

        def barrier():
            toks = [sq.tok(), sg.tok(), sT.tok(), sV.tok(), sS.tok(), sC.tok()]
            for e in (nc.sync, nc.tensor, nc.vector, nc.scalar, nc.gpsimd):
                W(e, *toks)

        _sid_P0 = nc.enter_named_scope("P0", False)[0]
        for out, in_ in [(c_idf, d_idf), (c_ones, d_ones), (c_S8f, d_S8),
                         (c_S8T, d_S8T)]:
            sq.inc(nc.sync.dma_start(out=out[:], in_=in_[:]), 16)
        sg.inc(nc.gpsimd.dma_start(out=c_cw[:], in_=d_cw[:]), 16)
        sg.inc(nc.gpsimd.dma_start(out=c_idb[:], in_=d_idf[:]), 16)
        tok_consts = sq.tok()
        tok_gpc = sg.tok()

        with contextlib.ExitStack() as st0:
            fwst = st0.enter_context(nc.sbuf_tensor("fwst", [128, 2 * 2048], F32))
            fq8 = st0.enter_context(nc.sbuf_tensor("fq8", [128, 2048], FP8))
            fbf = st0.enter_context(nc.sbuf_tensor("fbf", [128, 2048], BF16))
            fam = st0.enter_context(nc.sbuf_tensor("fam", [128, 16], F32))
            famT = st0.enter_context(nc.sbuf_tensor("famT", [16, 128], F32))
            fsc = st0.enter_context(nc.sbuf_tensor("fsc", [16, 8], F32))
            fsr = st0.enter_context(nc.sbuf_tensor("fsr", [1, 32], F32))
            fscB = st0.enter_context(nc.sbuf_tensor("fscB", [128, 32], F32))
            ld = [None] * DCH
            tq_toks = [None] * DCH
            tok_evac = None
            tok_st = None
            W(nc.tensor, tok_consts, tok_gpc)
            for r in range(2):
                ld[r] = sq.inc(nc.sync.dma_start(
                    out=fwst[:, 2048 * (r % 2):2048 * (r % 2) + 2048],
                    in_=d_fw[128 * r:128 * r + 128, :]), 16)
            for r in range(DCH):
                buf = fwst[:, 2048 * (r % 2):2048 * (r % 2) + 2048]
                W(nc.vector, ld[r], tok_consts)
                nc.vector.reduce_max(
                    fam[:], buf.rearrange("p (g k) -> p g k", g=16), axis=AXX,
                    apply_absolute_value=True)
                tam = sV.inc(nc.vector.drain())
                W(nc.tensor, tam, tok_evac)
                ttr = sT.inc(nc.tensor.transpose(pg[0:16, 0:128], fam[:], c_idf[:]))
                W(nc.vector, ttr)
                sV.inc(nc.vector.tensor_copy(famT[:, :], pg[0:16, 0:128]))
                nc.vector.drain()
                sV.inc(nc.vector.reduce_max(fsc[:, 0:1], famT[:], axis=AXX))
                nc.vector.drain()
                sV.inc(nc.vector.tensor_scalar(fsc[:, 4:5], fsc[:, 0:1], 1e-4,
                                               1.0 / 224.0, OP.max, OP.mult))
                nc.vector.drain()
                nc.vector.reciprocal(fsc[:, 3:4], fsc[:, 4:5])
                tsc = sV.inc(nc.vector.drain())
                W(nc.tensor, tsc)
                sT.inc(nc.tensor.transpose(pg[0:1, 128:144], fsc[:, 3:4],
                                           c_idf[0:16, 0:16]))
                ttr2 = sT.inc(nc.tensor.transpose(pg[0:1, 176:192], fsc[:, 4:5],
                                                  c_idf[0:16, 0:16]))
                W(nc.vector, ttr2)
                nc.vector.tensor_copy(fsr[:, 0:16], pg[0:1, 128:144])
                nc.vector.tensor_copy(fsr[:, 16:32], pg[0:1, 176:192])
                tvr = sV.inc(nc.vector.drain())
                W(nc.tensor, tvr)
                sT.inc(nc.tensor.matmul(pg[0:128, 144:160], c_ones[:], fsr[0:1, 0:16],
                                        start=True, stop=True))
                tb = sT.inc(nc.tensor.matmul(pg[0:128, 160:176], c_ones[:], fsr[0:1, 16:32],
                                             start=True, stop=True))
                W(nc.vector, tb)
                nc.vector.tensor_copy(fscB[:, 0:16], pg[0:128, 144:160])
                nc.vector.tensor_copy(fscB[:, 16:32], pg[0:128, 160:176])
                tok_evac = sV.inc(nc.vector.drain())
                W(nc.vector, tok_st)
                sV.inc(nc.vector.tensor_tensor(
                    fq8[:].rearrange("p (g k) -> p g k", g=16),
                    buf.rearrange("p (g k) -> p g k", g=16),
                    fscB[:, 0:16].unsqueeze(-1).broadcast_to((128, 16, 128)), OP.mult))
                tq = sV.inc(nc.vector.tensor_tensor(
                    fbf[:].rearrange("p (g k) -> p g k", g=16),
                    fq8[:].rearrange("p (g k) -> p g k", g=16),
                    fscB[:, 16:32].unsqueeze(-1).broadcast_to((128, 16, 128)), OP.mult))
                tq_toks[r] = tq
                W(nc.sync, tq)
                tok_st = sq.inc(nc.sync.dma_start(out=s_wd[128 * r:128 * r + 128, :],
                                                  in_=fbf[:]), 16)
                if r + 2 < DCH:
                    ld[r + 2] = sq.inc(nc.sync.dma_start(
                        out=fwst[:, 2048 * (r % 2):2048 * (r % 2) + 2048],
                        in_=d_fw[128 * (r + 2):128 * (r + 2) + 128, :]), 16)
        barrier()
        nc.leave_named_scope("P0", _sid_P0, False)

        _sid_P1 = nc.enter_named_scope("P1", False)[0]
        tok_colsum_mm = None
        with contextlib.ExitStack() as st1:
            for d in range(DCH):
                sg.inc(nc.gpsimd.dma_start(out=wbuf[:, 512 * d:512 * d + 512],
                                           in_=d_wk[128 * d:128 * d + 128, :]), 16)
                sg.inc(nc.gpsimd.dma_start(
                    out=wbuf[:, 8192 + 512 * d:8192 + 512 * d + 512],
                    in_=d_wv[128 * d:128 * d + 128, :]), 16)
            tok_wkv = sg.tok()

            xbuf = st1.enter_context(nc.sbuf_tensor("xbuf", [128, 4 * 2048], F32))
            xh1f = st1.enter_context(nc.sbuf_tensor("xh1f", [128, 2 * 2048], F32))
            sqd = st1.enter_context(nc.sbuf_tensor("sqd", [128, 2048], F32))
            statS = st1.enter_context(nc.sbuf_tensor("statS", [128, 4], F32))
            statQ = st1.enter_context(nc.sbuf_tensor("statQ", [128, 4], F32))
            statM = st1.enter_context(nc.sbuf_tensor("statM", [128, 16], F32))
            xfm = st1.enter_context(nc.sbuf_tensor("xfm", [128, 2 * DCH * 512], BF16,
                                                   is_transpose_dest=True))
            kro = st1.enter_context(nc.sbuf_tensor("kro", [128, 4 * 512], BF16))
            krt1 = st1.enter_context(nc.sbuf_tensor("krt1", [128, 2 * 512], BF16))
            vfm = st1.enter_context(nc.sbuf_tensor("vfm", [128, 4 * 512], BF16))
            eab = st1.enter_context(nc.sbuf_tensor("eab", [128, 8], F32))
            gsb = st1.enter_context(nc.sbuf_tensor("gsb", [8, 24], F32))
            usb = st1.enter_context(nc.sbuf_tensor("usb", [128, 8], F32))
            wsb = st1.enter_context(nc.sbuf_tensor("wsb", [128, 4], F32))
            kwvw = st1.enter_context(nc.sbuf_tensor("kwvw", [128, 1024], BF16))
            c_Sroll = st1.enter_context(nc.sbuf_tensor("c_Sroll", [128, 16 * 128], BF16))
            c_ecol = st1.enter_context(nc.sbuf_tensor("c_ecol", [128, 16], F32))
            tcos = st1.enter_context(nc.sbuf_tensor("tcos", [128, 2 * ST], F32))
            tsin = st1.enter_context(nc.sbuf_tensor("tsin", [128, 2 * ST], F32))
            csacc = st1.enter_context(nc.sbuf_tensor("csacc", [4, 512], F32))

            sg.inc(nc.gpsimd.dma_start(out=c_Sroll[:], in_=d_Sroll[:]), 16)
            tok_sroll = sg.tok()
            sq.inc(nc.sync.dma_start(out=c_ecol[:], in_=d_ecol[:]), 16)
            tok_ecol = sq.tok()

            colsum_started = [False]
            dbgdma = {}
            csaddV = [None]
            finq = [None]
            prev_w_tok = [None]
            prev_ck_evac = [None]
            prev_sab_user = [None]
            prev_texp = [None]
            tok_tab = [None] * NST
            tok_xload = [[None] * NSUB for _ in range(NST)]
            tok_scr = [None] * NST
            tok_xfm = [None] * NST
            tok_ropeV = [None] * NST
            tok_kvT = [None] * NST
            normS = [None] * (TH // SUB)
            colsumT = [None] * (TH // SUB)
            scrgp = [None] * (TH // SUB)

            for t in range(NST):
                par = t % 2
                if t >= 2:
                    W(nc.sync, tok_ropeV[t - 2])
                sq.inc(nc.sync.dma_start(out=tcos[:, ST * par:ST * par + ST],
                                         in_=d_cc[:, ST * t:ST * t + ST]), 16)
                tok_tab[t] = sq.inc(nc.sync.dma_start(out=tsin[:, ST * par:ST * par + ST],
                                                      in_=d_ss[:, ST * t:ST * t + ST]), 16)
                for s in range(NSUB):
                    u = NSUB * t + s
                    if u >= 4:
                        W(nc.sync, normS[u - 4])
                    tok_xload[t][s] = sq.inc(nc.sync.dma_start(
                        out=xbuf[:, 2048 * (u % 4):2048 * (u % 4) + 2048],
                        in_=d_x[SUB * u:SUB * u + SUB, :]), 16)
                for s in range(NSUB):
                    u = NSUB * t + s
                    bx = xbuf[:, 2048 * (u % 4):2048 * (u % 4) + 2048]
                    W(nc.vector, tok_xload[t][s])
                    sV.inc(nc.vector.tensor_scalar(bx, bx, 1.0, None, OP.mult, OP.add,
                                                   accum_out=statS[:, s:s + 1]))
                    W(nc.scalar, tok_xload[t][s], finq[0])
                    sS.inc(nc.scalar.activation(sqd[:], bx, AF.Square,
                                                accum_out=statQ[:, s:s + 1]))
                sS.inc(nc.scalar.drain())
                nc.vector.drain()
                W(nc.vector, sS.tok())
                sV.inc(nc.vector.tensor_scalar(statM[:, 0:4], statS[:], 1.0 / D, None, OP.mult))
                nc.vector.drain()
                finq[0] = sV.inc(nc.vector.tensor_scalar(statM[:, 4:8], statQ[:], 1.0 / D,
                                                          None, OP.mult))
                nc.vector.drain()
                sV.inc(nc.vector.tensor_tensor(statM[:, 8:12], statM[:, 0:4],
                                               statM[:, 0:4], OP.mult))
                nc.vector.drain()
                tvv = sV.inc(nc.vector.scalar_tensor_tensor(
                    statM[:, 8:12], statM[:, 4:8], EPS, statM[:, 8:12],
                    OP.add, OP.subtract))
                nc.vector.drain()
                W(nc.scalar, tvv)
                nc.scalar.activation(statM[:, 4:8], statM[:, 8:12], AF.Sqrt)
                tsd = sS.inc(nc.scalar.drain())
                W(nc.vector, tsd)
                nc.vector.reciprocal(statM[:, 8:12], statM[:, 4:8])
                trv = sV.inc(nc.vector.drain())
                nc.vector.scalar_tensor_tensor(
                    statM[:, 12:16], statM[:, 0:4], -1.0, statM[:, 8:12], OP.mult, OP.mult)
                tmb = sV.inc(nc.vector.drain())
                for s in range(NSUB):
                    u = NSUB * t + s
                    bx = xbuf[:, 2048 * (u % 4):2048 * (u % 4) + 2048]
                    bh = xh1f[:, 2048 * (u % 2):2048 * (u % 2) + 2048]
                    W(nc.vector, trv)
                    if u >= 2:
                        W(nc.vector, colsumT[u - 2], scrgp[u - 2], dbgdma.get(u - 2))
                    tvsub = sV.inc(nc.vector.tensor_scalar(
                        bh, bx, statM[:, 0 + s:1 + s], None, OP.subtract))
                    W(nc.scalar, tvsub)
                    th = sS.inc(nc.scalar.activation(bh, bh, AF.Identity,
                                                     scale=statM[:, 8 + s:9 + s]))
                    normS[u] = th
                    W(nc.tensor, th, tok_ecol, csaddV[0])
                    for c in range(4):
                        tk = sT.inc(nc.tensor.matmul(
                            pg[0:4, 0:512], c_ecol[:, 4 * c:4 * c + 4],
                            bh[:, 512 * c:512 * c + 512], start=(c == 0), stop=(c == 3)))
                        colsumT[u] = tk
                    W(nc.vector, colsumT[u])
                    if u == 0:
                        nc.vector.tensor_copy(csacc[:, :], pg[0:4, 0:512])
                    else:
                        nc.vector.tensor_tensor(
                            csacc[:, :], csacc[:, :], pg[0:4, 0:512], OP.add)
                    csaddV[0] = sV.inc(nc.vector.drain())
                    if u == TH // SUB - 1:
                        tok_colsum_mm = csaddV[0]
                    if dbg and s == NSUB - 1:
                        W(nc.sync, csaddV[0])
                        sq.inc(nc.sync.dma_start(out=d_dbg[404 + 4 * t:408 + 4 * t, 0:512],
                                                 in_=csacc[:, :]), 16)
                    W(nc.gpsimd, th, colsumT[u])
                    scrgp[u] = tok_scr[t] = sg.inc(nc.gpsimd.dma_start(
                        out=s_xh1[SUB * u:SUB * u + SUB, :], in_=bh), 16)
                    if dbg and t == 3:
                        W(nc.sync, th)
                        dbgdma[u] = sq.inc(nc.sync.dma_start(
                            out=d_dbg2[SUB * s:SUB * s + SUB, :], in_=bh), 16)
                W(nc.sync, tok_scr[t])
                if t >= 2:
                    W(nc.sync, tok_kvT[t - 2])
                fo1 = DCH * 512 * (t % 2)
                for c in range(DCH):
                    tok_xfm[t] = sq.inc(nc.sync.dma_start(
                        out=xfm[:, fo1 + 512 * c:fo1 + 512 * c + 512],
                        in_=s_xh1[ST * t:ST * t + ST, 128 * c:128 * c + 128],
                        transpose=True), 16)
                W(nc.tensor, tok_wkv, tok_xfm[t], prev_w_tok[0], prev_sab_user[0])
                for half in range(2):
                    tA = tB = None
                    for d in range(DCH):
                        o = 512 * d + 128 * (2 * half)
                        tA = sT.inc(nc.tensor.matmul(
                            p2[:, :], wbuf[:, o:o + 128], xfm[:, fo1 + 512 * d:fo1 + 512 * d + 512],
                            start=(d == 0), stop=(d == DCH - 1)))
                    for d in range(DCH):
                        o = 512 * d + 128 * (2 * half + 1)
                        tB = sT.inc(nc.tensor.matmul(
                            p3[:, :], wbuf[:, o:o + 128], xfm[:, fo1 + 512 * d:fo1 + 512 * d + 512],
                            start=(d == 0), stop=(d == DCH - 1)))
                    W(nc.vector, tA, tB, tok_tab[t])
                    cc_ = tcos[:, ST * par:ST * par + ST]
                    ss_ = tsin[:, ST * par:ST * par + ST]
                    a0 = 512 * (2 * half)
                    b0 = 512 * (2 * half + 1)
                    sV.inc(nc.vector.tensor_tensor(krt1[:, 0:512], p2[:, :], cc_, OP.mult))
                    sV.inc(nc.vector.tensor_tensor(krt1[:, 512:1024], p3[:, :], ss_, OP.mult))
                    sV.inc(nc.vector.tensor_tensor(kro[:, a0:a0 + 512], krt1[:, 0:512],
                                                   krt1[:, 512:1024], OP.subtract))
                    sV.inc(nc.vector.tensor_tensor(krt1[:, 0:512], p2[:, :], ss_, OP.mult))
                    sV.inc(nc.vector.tensor_tensor(krt1[:, 512:1024], p3[:, :], cc_, OP.mult))
                    trope = sV.inc(nc.vector.tensor_tensor(
                        kro[:, b0:b0 + 512], krt1[:, 0:512], krt1[:, 512:1024], OP.add))
                    tok_ropeV[t] = trope
                    W(nc.tensor, trope)
                for half in range(2):
                    tA = tB = None
                    for d in range(DCH):
                        o = 8192 + 512 * d + 128 * (2 * half)
                        tA = sT.inc(nc.tensor.matmul(
                            ps2[:, 0:512], wbuf[:, o:o + 128], xfm[:, fo1 + 512 * d:fo1 + 512 * d + 512],
                            start=(d == 0), stop=(d == DCH - 1)))
                    for d in range(DCH):
                        o = 8192 + 512 * d + 128 * (2 * half + 1)
                        tB = sT.inc(nc.tensor.matmul(
                            ps2[:, 512:1024], wbuf[:, o:o + 128],
                            xfm[:, fo1 + 512 * d:fo1 + 512 * d + 512],
                            start=(d == 0), stop=(d == DCH - 1)))
                    tok_kvT[t] = tB
                    W(nc.scalar, tA, tB)
                    sS.inc(nc.scalar.copy(vfm[:, 512 * (2 * half):512 * (2 * half) + 512],
                                          ps2[:, 0:512]))
                    tvc = sS.inc(nc.scalar.copy(
                        vfm[:, 512 * (2 * half + 1):512 * (2 * half + 1) + 512],
                        ps2[:, 512:1024]))
                    W(nc.tensor, tvc)
                for s in range(NSUB):
                    u = NSUB * t + s
                    tile = u // 16
                    mslot = u % 16
                    mm_tok = None
                    W(nc.tensor, tok_gpc, prev_texp[0])
                    for g2 in range(2):
                        for parb in range(2):
                            ch = 2 * g2 + parb
                            mm_tok = sT.inc(nc.tensor.matmul(
                                ps2[:, 4 * g2:4 * g2 + 4],
                                kro[:, 512 * ch + 128 * s:512 * ch + 128 * s + 128],
                                c_cw[:, 4 * parb:4 * parb + 4],
                                start=(g2 == 0 and parb == 0),
                                stop=(g2 == 1 and parb == 1)))
                    W(nc.scalar, mm_tok, prev_sab_user[0])
                    nc.scalar.activation(eab[:, :], ps2[:, 0:8], AF.Exp)
                    texp = sS.inc(nc.scalar.drain())
                    prev_texp[0] = texp
                    W(nc.tensor, texp, prev_sab_user[0])
                    tgs = sT.inc(nc.tensor.matmul(ps2[0:8, 512:520], c_S8f[:], eab[:, :],
                                                  start=True, stop=True))
                    W(nc.vector, tgs)
                    sV.inc(nc.vector.tensor_copy(gsb[:, 0:8], ps2[0:8, 512:520]))
                    nc.vector.drain()
                    sV.inc(nc.vector.reciprocal(gsb[:, 8:16], gsb[:, 0:8]))
                    nc.vector.drain()
                    nc.vector.tensor_scalar(gsb[:, 16:24], gsb[:, 8:16],
                                            0.5, None, OP.mult)
                    tgr = sV.inc(nc.vector.drain())
                    W(nc.tensor, tgr)
                    trt = sT.inc(nc.tensor.matmul(ps2[0:128, 520:528], c_S8T[:],
                                                  gsb[:, 16:24], start=True, stop=True))
                    W(nc.vector, trt)
                    sV.inc(nc.vector.tensor_tensor(usb[:, :], eab[:, :], ps2[:, 520:528],
                                                   OP.mult))
                    nc.vector.drain()
                    uview = usb[:].rearrange("p (g ab h) -> p g ab h", g=2, ab=2, h=2)
                    nc.vector.tensor_tensor(
                        wsb[:].rearrange("p (g h) -> p g h", g=2),
                        uview[:, :, 0, :], uview[:, :, 1, :], OP.add)
                    tw = sV.inc(nc.vector.drain())
                    prev_sab_user[0] = tw
                    W(nc.tensor, prev_w_tok[0])
                    ttr = None
                    for c in range(4):
                        ttr = sT.inc(nc.tensor.transpose(
                            pt8[:, 128 * c:128 * c + 128],
                            kro[:, 512 * c + 128 * s:512 * c + 128 * s + 128], c_idb[:]))
                    for c in range(4):
                        ttr = sT.inc(nc.tensor.transpose(
                            pt8[:, 512 + 128 * c:512 + 128 * c + 128],
                            vfm[:, 512 * c + 128 * s:512 * c + 128 * s + 128], c_idb[:]))
                    W(nc.vector, ttr, tw)
                    wk_view = wsb[:].rearrange("p (g h) -> p g h", g=2)
                    wkb = wk_view.unsqueeze(3).broadcast_to((128, 2, 2, 64))
                    for ab in range(2):
                        sV.inc(nc.vector.tensor_tensor(
                            kwvw[:, 0:512].rearrange("p (g ab h j) -> p g ab h j",
                                                     g=2, ab=2, h=2)[:, :, ab, :, :],
                            pt8[:, 0:512].rearrange("p (g ab h j) -> p g ab h j",
                                                    g=2, ab=2, h=2)[:, :, ab, :, :],
                            wkb, OP.mult))
                    wvb = wsb[:].unsqueeze(2).broadcast_to((128, 4, 128))
                    twv = sV.inc(nc.vector.tensor_tensor(
                        kwvw[:, 512:1024].rearrange("p (h j) -> p h j", h=4),
                        pt8[:, 512:1024].rearrange("p (h j) -> p h j", h=4),
                        wvb, OP.mult))
                    prev_w_tok[0] = twv
                    W(nc.tensor, twv, prev_ck_evac[0], tok_sroll)
                    stat = c_Sroll[:, 128 * mslot:128 * mslot + 128]
                    sT.inc(nc.tensor.matmul(p0[:, :], stat, kwvw[:, 0:512],
                                            start=(mslot == 0), stop=(mslot == 15)))
                    tcv = sT.inc(nc.tensor.matmul(p1[:, :], stat, kwvw[:, 512:1024],
                                                  start=(mslot == 0), stop=(mslot == 15)))
                    if mslot == 15:
                        W(nc.vector, tcv)
                        sV.inc(nc.vector.tensor_copy(ckacc[:, 512 * tile:512 * tile + 512],
                                                     p0[:, :]))
                        prev_ck_evac[0] = sV.inc(nc.vector.tensor_copy(
                            cvacc[:, 512 * tile:512 * tile + 512], p1[:, :]))
            W(nc.sync, tok_colsum_mm)
            tok_colsum_mm = sq.inc(nc.sync.dma_start(out=b_cs_i[:, :], in_=csacc[:, :]), 16)
            if dbg:
                sq.inc(nc.sync.dma_start(out=d_dbg[400:404, 0:512], in_=csacc[:, :]), 16)
        barrier()
        nc.leave_named_scope("P1", _sid_P1, False)

        _sid_P15 = nc.enter_named_scope("P15", False)[0]
        with contextlib.ExitStack() as st15:
            csb = st15.enter_context(nc.sbuf_tensor("csb", [4, 1024], F32))
            cst = st15.enter_context(nc.sbuf_tensor("cst", [4, 512], F32))
            csfm = st15.enter_context(nc.sbuf_tensor("csfm", [128, 16], F32))
            wikf = st15.enter_context(nc.sbuf_tensor("wikf", [128, DCH * 128], F32))
            wiqf = st15.enter_context(nc.sbuf_tensor("wiqf", [128, DCH * 128], F32))
            xfb = st15.enter_context(nc.sbuf_tensor("xfb", [128, 4 * 2048], F32))
            xffm = st15.enter_context(nc.sbuf_tensor("xffm", [128, DCH * 512], F32))
            kmb = st15.enter_context(nc.sbuf_tensor("kmb", [128, 2], F32))
            qif = st15.enter_context(nc.sbuf_tensor("qif", [128, 512], F32))
            srow = st15.enter_context(nc.sbuf_tensor("srow", [1, 512], F32))
            sbb = st15.enter_context(nc.sbuf_tensor("sbb", [128, 512], F32))
            tmpa = st15.enter_context(nc.sbuf_tensor("tmpa", [128, 2048], F32))
            scol = st15.enter_context(nc.sbuf_tensor("scol", [128, 4], F32))
            rnk = st15.enter_context(nc.sbuf_tensor("rnk", [128, 8], F32))
            ptb = st15.enter_context(nc.sbuf_tensor("ptb", [128, 128], F32))
            selb = st15.enter_context(nc.sbuf_tensor("selb", [64, 1024], F32))
            selt = st15.enter_context(nc.sbuf_tensor("selt", [64, 2048], F32))
            st_q = st15.enter_context(nc.sbuf_tensor("st_q", [128, 8], F32))
            st_s = st15.enter_context(nc.sbuf_tensor("st_s", [128, 16], F32))
            c_iota = st15.enter_context(nc.sbuf_tensor("c_iota", [128, 64], F32))
            c_LT = st15.enter_context(nc.sbuf_tensor("c_LT", [128, 2 * NB], F32))
            c_SELC = st15.enter_context(nc.sbuf_tensor("c_SELC", [128, 2 * NB], F32))

            for out, in_ in [(c_iota, d_iota), (c_LT, d_LT), (c_SELC, d_SELC)]:
                sq.inc(nc.sync.dma_start(out=out[:], in_=in_[:]), 16)
            for d in range(DCH):
                sq.inc(nc.sync.dma_start(out=wikf[:, 128 * d:128 * d + 128],
                                         in_=d_wik[128 * d:128 * d + 128, :]), 16)
                sq.inc(nc.sync.dma_start(out=wiqf[:, 128 * d:128 * d + 128],
                                         in_=d_wiq[128 * d:128 * d + 128, :]), 16)
            for s in range(4):
                sq.inc(nc.sync.dma_start(out=xfb[:, 2048 * s:2048 * s + 2048],
                                         in_=d_xf[128 * s:128 * s + 128, :]), 16)
            tok_lds = sq.tok()

            W(nc.gpsimd, tok_colsum_mm)
            tcc = sC.inc(nc.gpsimd.collective_compute(
                "AllGather", OP.bypass, replica_groups=[[0, 1], [2, 3], [4, 5], [6, 7]],
                ins=[b_cs_i.opt()], outs=[b_cs_o.opt()]))
            W(nc.sync, tcc)
            sq.inc(nc.sync.dma_start(out=csb[:, 0:512], in_=b_cs_o[0:4, :]), 16)
            tcr = sq.inc(nc.sync.dma_start(out=csb[:, 512:1024], in_=b_cs_o[4:8, :]), 16)
            W(nc.vector, tcr)
            tct = sV.inc(nc.vector.tensor_tensor(cst[:, :], csb[:, 0:512],
                                                 csb[:, 512:1024], OP.add))
            W(nc.tensor, tct)
            ttj = None
            for m in range(4):
                ttj = sT.inc(nc.tensor.transpose(
                    pg[0:128, 4 * m:4 * m + 4], cst[0:4, 128 * m:128 * m + 128],
                    c_idf[0:4, 0:4]))
            W(nc.vector, ttj)
            nc.vector.tensor_copy(csfm[:, :], pg[0:128, 0:16])
            tcf = sV.inc(nc.vector.drain())
            W(nc.tensor, tcf, tok_lds)
            tkm = None
            for d in range(DCH):
                col = 4 * (d % 4) + d // 4
                tkm = sT.inc(nc.tensor.matmul(ps2[0:128, 0:1],
                                              wikf[:, 128 * d:128 * d + 128],
                                              csfm[:, col:col + 1], start=(d == 0),
                                              stop=(d == DCH - 1)))
            W(nc.scalar, tkm)
            nc.scalar.activation(kmb[:, 0:1], ps2[:, 0:1], AF.Identity, scale=1.0 / T)
            tkme = sS.inc(nc.scalar.drain())
            for s in range(4):
                bx = xfb[:, 2048 * s:2048 * s + 2048]
                W(nc.vector, tok_lds)
                sV.inc(nc.vector.tensor_scalar(bx, bx, 1.0, None, OP.mult, OP.add,
                                               accum_out=st_q[:, s:s + 1]))
                W(nc.scalar, tok_lds)
                sS.inc(nc.scalar.activation(tmpa[:], bx, AF.Square,
                                            accum_out=st_q[:, 4 + s:5 + s]))
            sS.inc(nc.scalar.drain())
            nc.vector.drain()
            W(nc.vector, sS.tok())
            sV.inc(nc.vector.tensor_scalar(st_s[:, 0:4], st_q[:, 0:4], 1.0 / D, None, OP.mult))
            nc.vector.drain()
            sV.inc(nc.vector.tensor_scalar(st_s[:, 4:8], st_q[:, 4:8], 1.0 / D, None, OP.mult))
            nc.vector.drain()
            sV.inc(nc.vector.tensor_tensor(st_s[:, 8:12], st_s[:, 0:4], st_s[:, 0:4],
                                           OP.mult))
            nc.vector.drain()
            tvv = sV.inc(nc.vector.tensor_tensor(st_s[:, 8:12], st_s[:, 4:8],
                                                 st_s[:, 8:12], OP.subtract))
            nc.vector.drain()
            tvv = sV.inc(nc.vector.tensor_scalar(st_s[:, 8:12], st_s[:, 8:12],
                                                 EPS, None, OP.add))
            nc.vector.drain()
            W(nc.scalar, tvv)
            nc.scalar.activation(st_s[:, 4:8], st_s[:, 8:12], AF.Sqrt)
            tsd = sS.inc(nc.scalar.drain())
            W(nc.vector, tsd)
            nc.vector.reciprocal(st_s[:, 8:12], st_s[:, 4:8])
            trv = sV.inc(nc.vector.drain())
            nc.vector.scalar_tensor_tensor(
                st_s[:, 12:16], st_s[:, 0:4], -1.0, st_s[:, 8:12], OP.mult, OP.mult)
            tmb = sV.inc(nc.vector.drain())
            tnorm = None
            for s in range(4):
                bx = xfb[:, 2048 * s:2048 * s + 2048]
                W(nc.vector, trv)
                tvsub = sV.inc(nc.vector.tensor_scalar(
                    bx, bx, st_s[:, 0 + s:1 + s], None, OP.subtract))
                W(nc.scalar, tvsub)
                tnorm = sS.inc(nc.scalar.activation(bx, bx, AF.Identity,
                                                    scale=st_s[:, 8 + s:9 + s]))
            W(nc.tensor, tnorm)
            for c in range(DCH):
                ttj = None
                for s in range(4):
                    ttj = sT.inc(nc.tensor.transpose(
                        pg[0:128, 128 * s:128 * s + 128],
                        xfb[:, 2048 * s + 128 * c:2048 * s + 128 * c + 128], c_idf[:]))
                W(nc.vector, ttj)
                tve = sV.inc(nc.vector.tensor_copy(xffm[:, 512 * c:512 * c + 512],
                                                   pg[0:128, 0:512]))
                W(nc.tensor, tve)
            tqi = None
            W(nc.tensor, tkme)
            for d in range(DCH):
                tqi = sT.inc(nc.tensor.matmul(ps2[:, 0:512],
                                              wiqf[:, 128 * d:128 * d + 128],
                                              xffm[:, 512 * d:512 * d + 512],
                                              start=(d == 0), stop=(d == DCH - 1)))
            W(nc.vector, tqi, tkme)
            tqe = sV.inc(nc.vector.tensor_copy(qif[:, :], ps2[:, 0:512]))
            W(nc.tensor, tqe)
            tsc = sT.inc(nc.tensor.matmul(pg[0:1, 0:512], kmb[:, 0:1], qif[:, :],
                                          start=True, stop=True))
            W(nc.vector, tsc)
            tsr = sV.inc(nc.vector.tensor_copy(srow[:, :], pg[0:1, 0:512]))
            W(nc.tensor, tsr)
            tbb = sT.inc(nc.tensor.matmul(ps2[0:128, 0:512], c_ones[:], srow[:, :],
                                          start=True, stop=True))
            W(nc.vector, tbb)
            tsbb = sV.inc(nc.vector.tensor_copy(sbb[:, :], ps2[:, 0:512]))
            tpt = None
            for c in range(2):
                sV.inc(nc.vector.tensor_tensor(tmpa[:, 0:512], sbb[:, :],
                                               c_SELC[:, 512 * c:512 * c + 512], OP.mult))
                sV.inc(nc.vector.reduce_sum(scol[:, c:c + 1], tmpa[:, 0:512], axis=AXX))
                nc.vector.drain()
                sV.inc(nc.vector.tensor_scalar(tmpa[:, 0:512], sbb[:, :],
                                               scol[:, c:c + 1], None, OP.is_gt))
                sV.inc(nc.vector.tensor_scalar(tmpa[:, 512:1024], sbb[:, :],
                                               scol[:, c:c + 1], None, OP.is_equal))
                sV.inc(nc.vector.tensor_tensor(tmpa[:, 512:1024], tmpa[:, 512:1024],
                                               c_LT[:, 512 * c:512 * c + 512], OP.mult))
                sV.inc(nc.vector.tensor_tensor(tmpa[:, 0:512], tmpa[:, 0:512],
                                               tmpa[:, 512:1024], OP.add))
                sV.inc(nc.vector.reduce_sum(rnk[:, c:c + 1], tmpa[:, 0:512], axis=AXX))
                nc.vector.drain()
                nc.vector.tensor_scalar(ptb[:, 64 * c:64 * c + 64],
                                        c_iota[:], rnk[:, c:c + 1], None, OP.is_equal)
                tpt = sV.inc(nc.vector.drain())
            W(nc.tensor, tpt)
            tgk = None
            for c in range(2):
                tgk = sT.inc(nc.tensor.matmul(ps2[0:64, 0:512], ptb[:, 64 * c:64 * c + 64],
                                              ckacc[:, 512 * c:512 * c + 512],
                                              start=(c == 0), stop=(c == 1)))
            for c in range(2):
                tgk = sT.inc(nc.tensor.matmul(ps2[0:64, 512:1024],
                                              ptb[:, 64 * c:64 * c + 64],
                                              cvacc[:, 512 * c:512 * c + 512],
                                              start=(c == 0), stop=(c == 1)))
            W(nc.vector, tgk)
            sV.inc(nc.vector.tensor_copy(selb[:, 0:512], ps2[0:64, 0:512]))
            tse = sV.inc(nc.vector.tensor_copy(selb[:, 512:1024], ps2[0:64, 512:1024]))
            W(nc.sync, tse)
            tsw = sq.inc(nc.sync.dma_start(out=b_sel_i[:, :], in_=selb[:, :]), 16)
            W(nc.gpsimd, tsw)
            tcc2 = sC.inc(nc.gpsimd.collective_compute(
                "AllGather", OP.bypass, replica_groups=[[0, 1], [2, 3], [4, 5], [6, 7]],
                ins=[b_sel_i.opt()], outs=[b_sel_o.opt()]))
            W(nc.sync, tcc2)
            sq.inc(nc.sync.dma_start(out=selt[:, 0:1024], in_=b_sel_o[0:64, :]), 16)
            tsr2 = sq.inc(nc.sync.dma_start(out=selt[:, 1024:2048], in_=b_sel_o[64:128, :]), 16)
            W(nc.vector, tsr2)
            tst = sV.inc(nc.vector.tensor_tensor(selb[:, :], selt[:, 0:1024],
                                                 selt[:, 1024:2048], OP.add))
            W(nc.gpsimd, tst)
            tmz = sg.inc(nc.gpsimd.memset(movab[:], 0.0))
            sg.inc(nc.gpsimd.dma_start(out=cv2[0:64, :], in_=selb[:, 512:1024]), 16)
            tcv2 = sg.inc(nc.gpsimd.dma_start(out=cv2[64:128, :], in_=selb[:, 512:1024]), 16)
            W(nc.tensor, tst)
            for kv in range(4):
                for pr in range(2):
                    col = 256 * (kv // 2) + 64 * (kv % 2) + 128 * pr
                    ttp = sT.inc(nc.tensor.transpose(
                        pg[0:64, 0:64] if pr == 0 else pg[0:64, 64:128],
                        selb[:, col:col + 64], c_idf[0:64, 0:64]))
                W(nc.vector, ttp, tmz)
                m0 = 128 * (2 * kv + 0)
                m1 = 128 * (2 * kv + 1)
                sV.inc(nc.vector.tensor_copy(movab[0:64, m0:m0 + 64], pg[0:64, 0:64]))
                sV.inc(nc.vector.tensor_copy(movab[64:128, m0 + 64:m0 + 128], pg[0:64, 0:64]))
                sV.inc(nc.vector.tensor_copy(movab[0:64, m1:m1 + 64], pg[0:64, 64:128]))
                nc.vector.tensor_copy(movab[64:128, m1 + 64:m1 + 128], pg[0:64, 64:128])
                tmv = sV.inc(nc.vector.drain())
                W(nc.tensor, tmv)
            W(nc.vector, tcv2)
            if dbg:
                W(nc.sync, tst, tpt)
                sq.inc(nc.sync.dma_start(out=d_dbg[0:64, 0:1024], in_=selb[:, :]), 16)
                sq.inc(nc.sync.dma_start(out=d_dbg[64:65, 0:512], in_=srow[:, :]), 16)
                sq.inc(nc.sync.dma_start(out=d_dbg[66:194, 0:8], in_=rnk[:, :]), 16)
                sq.inc(nc.sync.dma_start(out=d_dbg[66:194, 8:12], in_=scol[:, :]), 16)
                sq.inc(nc.sync.dma_start(out=d_dbg[66:194, 16:144], in_=ptb[:, :]), 16)
                sq.inc(nc.sync.dma_start(out=d_dbg[66:194, 144:160], in_=csfm[:, :]), 16)
                sq.inc(nc.sync.dma_start(out=d_dbg[66:194, 160:162], in_=kmb[:, :]), 16)
                sq.inc(nc.sync.dma_start(out=d_dbg[195:199, 0:512], in_=cst[:, :]), 16)
                sq.inc(nc.sync.dma_start(out=d_dbg[199:199+128, 0:1024], in_=ckacc[:, :]), 16)
        barrier()
        nc.leave_named_scope("P15", _sid_P15, False)
        stCK.close()

        _sid_P2a1 = nc.enter_named_scope("P2a1", False)[0]
        with contextlib.ExitStack() as sta:
            for d in range(DCH):
                sg.inc(nc.gpsimd.dma_start(out=wbuf[:, 2048 * d:2048 * d + 2048],
                                           in_=d_wq[128 * d:128 * d + 128, :]), 16)
            tok_wq = sg.tok()
            xfm2 = sta.enter_context(nc.sbuf_tensor("xfm2", [128, 2 * DCH * 512], BF16,
                                                    is_transpose_dest=True))
            qro = sta.enter_context(nc.sbuf_tensor("qro", [128, DCH * 512], BF16))
            qrt = sta.enter_context(nc.sbuf_tensor("qrt", [128, 2 * 512], BF16))
            ef = sta.enter_context(nc.sbuf_tensor("ef", [128, 1024], F32))
            mxb = sta.enter_context(nc.sbuf_tensor("mxb", [128, 48], F32))
            attb = sta.enter_context(nc.sbuf_tensor("attb", [128, 2 * 1024], BF16))
            tcos = sta.enter_context(nc.sbuf_tensor("tcos2", [128, 2 * ST], F32))
            tsin = sta.enter_context(nc.sbuf_tensor("tsin2", [128, 2 * ST], F32))

            prev_soft = [None]
            tok_ropeV2 = [None] * NST
            tok_qT = [None] * NST
            attdma = [None] * (TH // SUB)
            rope_toks = [None, None]
            for t in range(NST):
                par = t % 2
                if t >= 2:
                    W(nc.sync, tok_ropeV2[t - 2])
                sq.inc(nc.sync.dma_start(out=tcos[:, ST * par:ST * par + ST],
                                         in_=d_cc[:, ST * t:ST * t + ST]), 16)
                ttab = sq.inc(nc.sync.dma_start(out=tsin[:, ST * par:ST * par + ST],
                                                in_=d_ss[:, ST * t:ST * t + ST]), 16)
                if t >= 2:
                    W(nc.sync, tok_qT[t - 2])
                fo2 = DCH * 512 * (t % 2)
                txf = None
                for c in range(DCH):
                    txf = sq.inc(nc.sync.dma_start(
                        out=xfm2[:, fo2 + 512 * c:fo2 + 512 * c + 512],
                        in_=s_xh1[ST * t:ST * t + ST, 128 * c:128 * c + 128],
                        transpose=True), 16)
                W(nc.tensor, tok_wq, txf)
                for g in range(8):
                    pa, pb = (p0, p1) if g % 2 == 0 else (p2, p3)
                    W(nc.tensor, rope_toks[g % 2])
                    tA = tB = None
                    for d in range(DCH):
                        o = 2048 * d + 128 * (2 * g)
                        tA = sT.inc(nc.tensor.matmul(
                            pa[:, :], wbuf[:, o:o + 128],
                            xfm2[:, fo2 + 512 * d:fo2 + 512 * d + 512],
                            start=(d == 0), stop=(d == DCH - 1)))
                    for d in range(DCH):
                        o = 2048 * d + 128 * (2 * g + 1)
                        tB = sT.inc(nc.tensor.matmul(
                            pb[:, :], wbuf[:, o:o + 128],
                            xfm2[:, fo2 + 512 * d:fo2 + 512 * d + 512],
                            start=(d == 0), stop=(d == DCH - 1)))
                    W(nc.vector, tA, tB, ttab)
                    cc_ = tcos[:, ST * par:ST * par + ST]
                    ss_ = tsin[:, ST * par:ST * par + ST]
                    a0 = 512 * (2 * g)
                    b0 = 512 * (2 * g + 1)
                    sV.inc(nc.vector.tensor_tensor(qrt[:, 0:512], pa[:, :], cc_, OP.mult))
                    sV.inc(nc.vector.tensor_tensor(qrt[:, 512:1024], pb[:, :], ss_, OP.mult))
                    sV.inc(nc.vector.tensor_tensor(qro[:, a0:a0 + 512], qrt[:, 0:512],
                                                   qrt[:, 512:1024], OP.subtract))
                    sV.inc(nc.vector.tensor_tensor(qrt[:, 0:512], pa[:, :], ss_, OP.mult))
                    sV.inc(nc.vector.tensor_tensor(qrt[:, 512:1024], pb[:, :], cc_, OP.mult))
                    rope_toks[g % 2] = sV.inc(nc.vector.tensor_tensor(
                        qro[:, b0:b0 + 512], qrt[:, 0:512], qrt[:, 512:1024], OP.add))
                    tok_ropeV2[t] = rope_toks[g % 2]
                    tok_qT[t] = tB
                for s in range(NSUB):
                    u = NSUB * t + s
                    W(nc.tensor, rope_toks[0], rope_toks[1], prev_soft[0])
                    tms = None
                    for g in range(8):
                        kv = g // 2
                        for pr in range(2):
                            qc = 512 * (2 * g + pr) + 128 * s
                            first = (g % 4 == 0) and pr == 0
                            last = (g % 4 == 3) and pr == 1
                            tms = sT.inc(nc.tensor.matmul(
                                ps2[:, 128 * g:128 * g + 128],
                                qro[:, qc:qc + 128],
                                movab[:, 128 * (2 * kv + pr):128 * (2 * kv + pr) + 128],
                                start=first, stop=last))
                    W(nc.vector, tms)
                    sV.inc(nc.vector.reduce_max(
                        mxb[:, 0:16], ps2[:].rearrange("p (h j) -> p h j", h=16), axis=AXX))
                    nc.vector.drain()
                    tsub = sV.inc(nc.vector.tensor_tensor(
                        ef[:].rearrange("p (h j) -> p h j", h=16),
                        ps2[:].rearrange("p (h j) -> p h j", h=16),
                        mxb[:, 0:16].unsqueeze(2).broadcast_to((128, 16, 64)), OP.subtract))
                    prev_soft[0] = tsub
                    W(nc.scalar, tsub)
                    tex = sS.inc(nc.scalar.activation(ef[:], ef[:], AF.Exp, scale=ISQ))
                    W(nc.vector, tex)
                    sV.inc(nc.vector.reduce_sum(
                        mxb[:, 16:32], ef[:].rearrange("p (h j) -> p h j", h=16), axis=AXX))
                    nc.vector.drain()
                    sV.inc(nc.vector.reciprocal(mxb[:, 32:48], mxb[:, 16:32]))
                    nc.vector.drain()
                    ab = attb[:, 1024 * (u % 2):1024 * (u % 2) + 1024]
                    if u >= 2:
                        W(nc.vector, attdma[u - 2])
                    tat = sV.inc(nc.vector.tensor_tensor(
                        ab.rearrange("p (h j) -> p h j", h=16),
                        ef[:].rearrange("p (h j) -> p h j", h=16),
                        mxb[:, 32:48].unsqueeze(2).broadcast_to((128, 16, 64)), OP.mult))
                    W(nc.sync, tat)
                    attdma[u] = sq.inc(nc.sync.dma_start(
                        out=s_att[SUB * u:SUB * u + SUB, :], in_=ab), 16)
        barrier()
        nc.leave_named_scope("P2a1", _sid_P2a1, False)

        _sid_P2a2 = nc.enter_named_scope("P2a2", False)[0]
        with contextlib.ExitStack() as stb:
            for d in range(DCH):
                sg.inc(nc.gpsimd.dma_start(out=wbuf[:, 2048 * d:2048 * d + 2048],
                                           in_=d_wo[128 * d:128 * d + 128, :]), 16)
            tok_wo = sg.tok()
            attin = stb.enter_context(nc.sbuf_tensor("attin", [128, 2 * 1024], BF16))
            attT = stb.enter_context(nc.sbuf_tensor("attT", [128, 8 * 512], BF16))
            vout = stb.enter_context(nc.sbuf_tensor("vout", [128, 16 * 512], BF16))
            aob = stb.enter_context(nc.sbuf_tensor("aob", [128, 2 * 2048], BF16))

            prev_vread = [None]
            prev_aodma = [None]
            prev_tev = [None]
            trT = [None] * (TH // SUB)
            for t in range(NST):
                tld = [None] * NSUB
                for s in range(NSUB):
                    u = NSUB * t + s
                    if u >= 2:
                        W(nc.sync, trT[u - 2])
                    tld[s] = sq.inc(nc.sync.dma_start(
                        out=attin[:, 1024 * (s % 2):1024 * (s % 2) + 1024],
                        in_=s_att[SUB * u:SUB * u + SUB, :]), 16)
                    W(nc.tensor, tld[s], prev_vread[0])
                    ttr = None
                    for g in range(8):
                        ttr = sT.inc(nc.tensor.transpose(
                            pt8[:, 128 * g:128 * g + 128],
                            attin[:, 1024 * (s % 2) + 128 * g:1024 * (s % 2) + 128 * g + 128],
                            c_idb[:]))
                    trT[u] = ttr
                    W(nc.scalar, ttr)
                    dst = attT[:].rearrange("p (g c) -> p g c", g=8)[:, :, 128 * s:128 * s + 128]
                    prev_vread[0] = sS.inc(nc.scalar.copy(
                        dst, pt8[:].rearrange("p (g c) -> p g c", g=8)))
                W(nc.tensor, prev_vread[0], prev_tev[0])
                ve = [None] * 4
                for h in range(H):
                    pq = PQ[h % 4]
                    W(nc.tensor, ve[h % 4])
                    r0 = 64 * (h % 2)
                    tvm = sT.inc(nc.tensor.matmul(
                        pq[:, :], cv2[r0:r0 + 64, 128 * (h // 4):128 * (h // 4) + 128],
                        attT[r0:r0 + 64, 512 * (h // 2):512 * (h // 2) + 512],
                        start=True, stop=True))
                    W(nc.scalar, tvm)
                    ve[h % 4] = sS.inc(nc.scalar.copy(vout[:, 512 * h:512 * h + 512],
                                                      pq[:, :]))
                W(nc.tensor, ve[0], ve[1], ve[2], ve[3], tok_wo)
                for s in range(NSUB):
                    u = NSUB * t + s
                    W(nc.tensor, prev_tev[0])
                    tos = None
                    for oc in range(4):
                        for c in range(DCH):
                            tos = sT.inc(nc.tensor.matmul(
                                PQ[oc][:, :],
                                vout[:, 512 * c + 128 * s:512 * c + 128 * s + 128],
                                wbuf[:, 2048 * c + 512 * oc:2048 * c + 512 * oc + 512],
                                start=(c == 0), stop=(c == DCH - 1)))
                    W(nc.scalar, tos, prev_aodma[0])
                    ao = aob[:, 2048 * (u % 2):2048 * (u % 2) + 2048]
                    tev = None
                    for oc in range(4):
                        tev = sS.inc(nc.scalar.copy(ao[:, 512 * oc:512 * oc + 512],
                                                    PQ[oc][:, :]))
                    prev_tev[0] = tev
                    W(nc.sync, tev)
                    prev_aodma[0] = sq.inc(nc.sync.dma_start(
                        out=s_ao[SUB * u:SUB * u + SUB, :], in_=ao), 16)
        barrier()
        nc.leave_named_scope("P2a2", _sid_P2a2, False)

        _sid_P2b = nc.enter_named_scope("P2b", False)[0]
        with contextlib.ExitStack() as stc:
            twd = None
            for i in range(DCH):
                twd = sq.inc(nc.sync.dma_start(
                    out=wbuf[:, 2048 * i:2048 * i + 2048],
                    in_=s_wd[:, 128 * i:128 * i + 128], transpose=True), 16)
            aor = stc.enter_context(nc.sbuf_tensor("aor", [128, 2 * 2048], BF16))
            xpr = stc.enter_context(nc.sbuf_tensor("xpr", [128, 8 * 2048], F32))
            xh2 = stc.enter_context(nc.sbuf_tensor("xh2", [128, 2048], F32))
            y8 = stc.enter_context(nc.sbuf_tensor("y8", [128, 2048], FP8))
            xdb = stc.enter_context(nc.sbuf_tensor("xdb", [128, 2048], BF16))
            xdfm = stc.enter_context(nc.sbuf_tensor("xdfm", [128, 2 * DCH * 512], BF16,
                                                    is_transpose_dest=True))
            ffnb = stc.enter_context(nc.sbuf_tensor("ffnb", [128, 2048], BF16))
            xn = stc.enter_context(nc.sbuf_tensor("xn", [128, 2 * 2048], F32))
            stS2 = stc.enter_context(nc.sbuf_tensor("stS2", [128, 4], F32))
            stQ2 = stc.enter_context(nc.sbuf_tensor("stQ2", [128, 4], F32))
            stM2 = stc.enter_context(nc.sbuf_tensor("stM2", [128, 16], F32))
            amb = stc.enter_context(nc.sbuf_tensor("amb", [128, 64], F32))

            tok_xd = [None] * NST
            tok_xdfm = [None] * NST
            prev_ffn_read = [None]
            prev_out_add = {}
            prev_ffne = [None]
            prev_outadd = [None]
            xpadd = [None] * (TH // SUB)
            xddma = [None] * (TH // SUB)
            outdma = [None] * (TH // SUB)
            prev_qS = [None]
            prevVs = {0: None, 1: None}
            prevGs = {0: None, 1: None}

            for t in range(NST + 1):
                if t < NST:
                    tldx = [None] * NSUB
                    tlda = [None] * NSUB
                    for s in range(2):
                        u = NSUB * t + s
                        if u >= 2:
                            W(nc.sync, xpadd[u - 2])
                        W(nc.sync, prev_out_add.get(u % 8))
                        tldx[s] = sq.inc(nc.sync.dma_start(
                            out=xpr[:, 2048 * (u % 8):2048 * (u % 8) + 2048],
                            in_=d_x[SUB * u:SUB * u + SUB, :]), 16)
                        tlda[s] = sq.inc(nc.sync.dma_start(
                            out=aor[:, 2048 * (u % 2):2048 * (u % 2) + 2048],
                            in_=s_ao[SUB * u:SUB * u + SUB, :]), 16)
                    for s in range(NSUB):
                        u = NSUB * t + s
                        slot = u % 8
                        xp = xpr[:, 2048 * slot:2048 * slot + 2048]
                        W(nc.vector, tldx[s], tlda[s])
                        sV.inc(nc.vector.tensor_tensor(
                            xp, xp,
                            aor[:, 2048 * (u % 2):2048 * (u % 2) + 2048], OP.add))
                        tst1 = sV.inc(nc.vector.tensor_scalar(
                            xp, xp, 1.0, None, OP.mult, OP.add,
                            accum_out=stS2[:, s:s + 1]))
                        xpadd[u] = tst1
                        if s + 2 < NSUB:
                            u2 = u + 2
                            W(nc.sync, xpadd[u])
                            W(nc.sync, prev_out_add.get(u2 % 8))
                            tldx[s + 2] = sq.inc(nc.sync.dma_start(
                                out=xpr[:, 2048 * (u2 % 8):2048 * (u2 % 8) + 2048],
                                in_=d_x[SUB * u2:SUB * u2 + SUB, :]), 16)
                            tlda[s + 2] = sq.inc(nc.sync.dma_start(
                                out=aor[:, 2048 * (u2 % 2):2048 * (u2 % 2) + 2048],
                                in_=s_ao[SUB * u2:SUB * u2 + SUB, :]), 16)
                        W(nc.scalar, tst1)
                        sS.inc(nc.scalar.activation(xh2[:], xp, AF.Square,
                                                    accum_out=stQ2[:, s:s + 1]))
                    sS.inc(nc.scalar.drain())
                    nc.vector.drain()
                    W(nc.vector, sS.tok())
                    sV.inc(nc.vector.tensor_scalar(stM2[:, 0:4], stS2[:], 1.0 / D, None,
                                                   OP.mult))
                    nc.vector.drain()
                    sV.inc(nc.vector.tensor_scalar(stM2[:, 4:8], stQ2[:], 1.0 / D, None,
                                                   OP.mult))
                    nc.vector.drain()
                    sV.inc(nc.vector.tensor_tensor(stM2[:, 8:12], stM2[:, 0:4],
                                                   stM2[:, 0:4], OP.mult))
                    nc.vector.drain()
                    tvv = sV.inc(nc.vector.scalar_tensor_tensor(
                        stM2[:, 8:12], stM2[:, 4:8], EPS, stM2[:, 8:12],
                        OP.add, OP.subtract))
                    nc.vector.drain()
                    W(nc.scalar, tvv)
                    nc.scalar.activation(stM2[:, 4:8], stM2[:, 8:12], AF.Sqrt)
                    tsd = sS.inc(nc.scalar.drain())
                    W(nc.vector, tsd)
                    nc.vector.reciprocal(stM2[:, 8:12], stM2[:, 4:8])
                    trv = sV.inc(nc.vector.drain())
                    nc.vector.scalar_tensor_tensor(
                        stM2[:, 12:16], stM2[:, 0:4], -1.0, stM2[:, 8:12],
                        OP.mult, OP.mult)
                    tmb = sV.inc(nc.vector.drain())
                    for s in range(NSUB):
                        u = NSUB * t + s
                        slot = u % 8
                        par2 = u % 2
                        xp = xpr[:, 2048 * slot:2048 * slot + 2048]
                        xnv = xn[:, 2048 * par2:2048 * par2 + 2048]
                        W(nc.scalar, trv, tmb, prevVs[par2], prevGs[par2])
                        th2 = sS.inc(nc.scalar.activation(
                            xnv, xp, AF.Identity,
                            scale=stM2[:, 8 + s:9 + s], bias=stM2[:, 12 + s:13 + s]))
                        W(nc.vector, th2)
                        sV.inc(nc.vector.reduce_max(
                            amb[:, 0:16], xnv.rearrange("p (g k) -> p g k", g=16),
                            axis=AXX, apply_absolute_value=True))
                        tamx = sV.inc(nc.vector.drain())
                        prevVs[par2] = tamx
                        W(nc.vector, prev_qS[0])
                        sV.inc(nc.vector.tensor_scalar(amb[:, 32:48], amb[:, 0:16],
                                                       1e-4, 1.0 / 224.0,
                                                       OP.max, OP.mult))
                        nc.vector.drain()
                        sV.inc(nc.vector.reciprocal(amb[:, 16:32], amb[:, 32:48]))
                        tchain = sV.inc(nc.vector.drain())
                        xd = xdb[:, :]
                        W(nc.gpsimd, tchain)
                        if u >= 1:
                            W(nc.gpsimd, xddma[u - 1])
                        tq8 = sg.inc(nc.gpsimd.tensor_tensor(
                            y8[:].rearrange("p (g k) -> p g k", g=16),
                            xnv.rearrange("p (g k) -> p g k", g=16),
                            amb[:, 16:32].unsqueeze(2).broadcast_to((128, 16, 128)),
                            OP.mult), 1)
                        prevGs[par2] = tq8
                        txd = sg.inc(nc.gpsimd.tensor_tensor(
                            xd.rearrange("p (g k) -> p g k", g=16),
                            y8[:].rearrange("p (g k) -> p g k", g=16),
                            amb[:, 32:48].unsqueeze(2).broadcast_to((128, 16, 128)),
                            OP.mult), 1)
                        prev_qS[0] = txd
                        W(nc.sync, txd)
                        xddma[u] = tok_xd[t] = sq.inc(nc.sync.dma_start(
                            out=s_xd[SUB * u:SUB * u + SUB, :], in_=xd), 16)
                    W(nc.sync, tok_xd[t], prev_ffn_read[0])
                    fmo = DCH * 512 * (t % 2)
                    for c in range(DCH):
                        tok_xdfm[t] = sq.inc(nc.sync.dma_start(
                            out=xdfm[:, fmo + 512 * c:fmo + 512 * c + 512],
                            in_=s_xd[ST * t:ST * t + ST, 128 * c:128 * c + 128],
                            transpose=True), 16)
                tf = t - 1
                if tf >= 0:
                    W(nc.tensor, tok_xdfm[tf], twd)
                    ffo = DCH * 512 * (tf % 2)
                    for s in range(NSUB):
                        u = NSUB * tf + s
                        slot = u % 8
                        W(nc.tensor, prev_ffne[0])
                        tos = None
                        for oc in range(4):
                            for i in range(DCH):
                                tos = sT.inc(nc.tensor.matmul(
                                    PQ[oc][:, :],
                                    xdfm[:, ffo + 512 * i + 128 * s:ffo + 512 * i + 128 * s + 128],
                                    wbuf[:, 2048 * i + 512 * oc:2048 * i + 512 * oc + 512],
                                    start=(i == 0), stop=(i == DCH - 1)))
                        W(nc.scalar, tos, prev_outadd[0])
                        tev = None
                        for oc in range(4):
                            tev = sS.inc(nc.scalar.copy(ffnb[:, 512 * oc:512 * oc + 512],
                                                        PQ[oc][:, :]))
                        prev_ffne[0] = tev
                        ob = xpr[:, 2048 * slot:2048 * slot + 2048]
                        W(nc.vector, tev)
                        tadd = sV.inc(nc.vector.tensor_tensor(
                            ob, ob, ffnb[:], OP.add))
                        prev_outadd[0] = tadd
                        W(nc.sync, tadd)
                        outdma[u] = sq.inc(nc.sync.dma_start(
                            out=d_out[SUB * u:SUB * u + SUB, :], in_=ob), 16)
                        prev_out_add[slot] = outdma[u]
                    prev_ffn_read[0] = (sT, sT.n)
            W(nc.sync, sq.tok())
            W(nc.vector, sq.tok())
            nc.leave_named_scope("P2b", _sid_P2b, False)
    return nc


_CACHE = {}


def kernel(**inputs):
    x = np.asarray(inputs["L0_attn_x"])
    wq = np.asarray(inputs["L0_attn_wq"])[:, _PQ]
    wk = np.asarray(inputs["L0_attn_wk"])[:, _PK]
    wv = np.asarray(inputs["L0_attn_wv"])
    wo = np.asarray(inputs["L0_attn_wo"])
    wiq = np.asarray(inputs["L0_attn_wiq"])
    wik = np.asarray(inputs["L0_attn_wik"])
    fw = np.asarray(inputs["L0_ffn_weight"])
    cwstat = _host_cwstat(inputs["L0_attn_cwa"], inputs["L0_attn_cwb"])

    if "nc" not in _CACHE:
        _CACHE["nc"] = build_program()
        _CACHE["tabs"] = [_host_tables(h) for h in range(2)]
    nc = _CACHE["nc"]
    tabs = _CACHE["tabs"]

    in_maps = []
    for c in range(8):
        b, h = c // 2, c % 2
        tab = tabs[h]
        m = dict(
            x=np.ascontiguousarray(x[b, h * TH:(h + 1) * TH, :]),
            xf=np.ascontiguousarray(x[b, 0:512, :]),
            wqp=np.ascontiguousarray(wq), wkp=np.ascontiguousarray(wk),
            wv=wv, wo=wo, wiq=wiq, wik=wik, fw=fw, cwstat=cwstat,
            coscos=tab["coscos"], sinsin=tab["sinsin"], iota64=tab["iota64"],
            LT=tab["LT"], SELC=tab["SELC"], S8=tab["S8"], S8T=tab["S8T"],
            Sroll=tab["Sroll"], ident_f=tab["ident_f"], ones_row=tab["ones_row"],
            ecol=tab["ecol"],
        )
        in_maps.append(m)

    res = run_bass_kernel_spmd(nc, in_maps, core_ids=list(range(8)))
    out = np.empty((B, T, D), dtype=np.float32)
    for c in range(8):
        b, h = c // 2, c % 2
        out[b, h * TH:(h + 1) * TH, :] = res.results[c]["out"]
    return out
